# version 03 (final)
# baseline (speedup 1.0000x reference)
import math
import numpy as np
import ml_dtypes
from contextlib import ExitStack

import concourse.bass as bass
import concourse.mybir as mybir
from concourse.bass_utils import run_bass_kernel_spmd

F32 = mybir.dt.float32
BF16 = mybir.dt.bfloat16
AF = mybir.ActivationFunctionType
ALU = mybir.AluOpType
AX = mybir.AxisListType

NCORES = 8
T = 2048
B, SEQ, D = 2, 8192, 1024
KD = D // 128
BFD = np.dtype(ml_dtypes.bfloat16)

_FP4_MIDS = [0.25, 0.75, 1.25, 1.75, 2.5, 3.5, 5.0]
_FP4_INCS = [0.5, 0.5, 0.5, 0.5, 1.0, 1.0, 2.0]



def _f32(a):
    return np.ascontiguousarray(np.asarray(a), dtype=np.float32)


def _pcol(v):
    v = _f32(v)
    n = v.shape[0] // 128
    return np.ascontiguousarray(v.reshape(n, 128).T)


def _wlay(w):
    w = _f32(w)
    K, M = w.shape
    kt = K // 128
    return np.ascontiguousarray(
        w.reshape(kt, 128, M).transpose(1, 0, 2).reshape(128, kt * M))


def _fm(x2d):
    x2d = np.asarray(x2d)
    t, d = x2d.shape
    kd = d // 128
    return np.ascontiguousarray(
        x2d.T.reshape(kd, 128, t).transpose(1, 0, 2).reshape(128, kd * t))


def _core_slice(x, c):
    b, r = c // 4, c % 4
    return x[b, r * T:(r + 1) * T, :]



class LP:
    def __init__(self, name):
        self.nc = bass.Bass("TRN2", target_bir_lowering=False,
                            detect_race_conditions=False)
        self.stack = ExitStack()
        self.sems = {}
        self.cnt = {}
        self.n_stage = 0

    def sem(self, name):
        if name not in self.sems:
            self.sems[name] = self.stack.enter_context(self.nc.semaphore(name))
            self.cnt[name] = 0
        return self.sems[name]

    def bump(self, name, amt=1):
        self.sem(name)
        self.cnt[name] += amt
        return self.sems[name], self.cnt[name]

    def sbuf(self, name, shape, dt):
        return self.stack.enter_context(self.nc.sbuf_tensor(name, shape, dt))

    def psum(self, name, shape, dt=F32):
        return self.stack.enter_context(self.nc.psum_tensor(name, shape, dt))

    def scope(self):
        from contextlib import ExitStack as _ES
        return _ES()

    def sb(self, stack, name, shape, dt):
        self._sbn = getattr(self, "_sbn", 0) + 1
        return stack.enter_context(self.nc.sbuf_tensor(f"{name}{self._sbn}", shape, dt))

    def dram_scratch(self, name, shape, dt):
        return self.nc.dram_tensor(name, shape, dt).ap()

    def dram_in(self, name, shape, dt):
        return self.nc.dram_tensor(name, shape, dt, kind="ExternalInput").ap()

    def dram_out(self, name, shape, dt):
        return self.nc.dram_tensor(name, shape, dt, kind="ExternalOutput").ap()

    def block(self):
        self.n_stage += 1
        return self.nc.Block(f"st{self.n_stage}")

    def setup_consts(self, values):
        t = self.sbuf("fconsts", [128, max(1, len(values))], F32)
        self.fence = self.sbuf("fence", [1, 8], F32)
        self.padt = self.sbuf("padt", [1, 256], F32)
        for i, v in enumerate(values):
            self.nc.const_aps.register(F32, float(v), t[:, i:i + 1])
        with self.block() as blk:
            def body(e):
                for i, v in enumerate(values):
                    e.memset(t[:, i:i + 1], float(v))
                e.memset(self.fence[:, :], 0.0)
                e.memset(self.padt[:, :], 0.0)
            blk.vector(body)

    def finc_v(self, e, sem, amt=1):
        return e.tensor_copy(self.fence[:, 0:1], self.fence[:, 1:2]).then_inc(sem, amt)

    def finc_s(self, e, sem, amt=1):
        return e.activation(self.fence[:, 2:3], self.fence[:, 3:4], AF.Identity).then_inc(sem, amt)

    def finc_p(self, e, sem, amt=1):
        return e.tensor_copy(self.fence[:, 4:5], self.fence[:, 5:6]).then_inc(sem, amt)

    def pad_v(self, e):
        e.tensor_copy(self.padt[:, :], self.padt[:, :])

    def pad_s(self, e):
        e.activation(self.padt[:, :], self.padt[:, :], AF.Identity)

    def finc_t(self, e, sem, amt=1):
        e.drain()
        return e.nop().then_inc(sem, amt)

    def close(self):
        self.stack.close()


def dma_in(lp, loads, engine="sync", sem=None):
    if sem is None:
        sem = "gld" if engine == "gpsimd" else "ld"
    s = lp.sem(sem)
    with lp.block() as blk:
        def body(e):
            for dst, src in loads:
                e.dma_start(out=dst, in_=src).then_inc(s, 16)
        getattr(blk, engine)(body)
    return lp.bump(sem, 16 * len(loads))[1]


def wait_stage(lp, engine, sem, tgt):
    with lp.block() as blk:
        def body(e):
            e.wait_ge(lp.sem(sem), tgt)
        getattr(blk, engine)(body)


def mm_stage(lp, *, lhsT, rhs, evac, tiles, k_tiles, psA, psB, name,
             pre_wait=None, out_free=512, out_part=128):
    mm = lp.sem("MMm")
    ev = lp.sem("MMe")
    mmb, evb = lp.cnt["MMm"], lp.cnt["MMe"]
    ps = [psA, psB]
    with lp.block() as blk:
        def t_body(e):
            for sname, tgt in (pre_wait or []):
                e.wait_ge(lp.sem(sname), tgt)
            for i, (m, n) in enumerate(tiles):
                if i >= 2:
                    e.wait_ge(ev, evb + i - 1)
                last = None
                for k in range(k_tiles):
                    last = e.matmul(ps[i % 2][0:out_part, 0:out_free], lhsT(k, m), rhs(k, n),
                                    start=(k == 0), stop=(k == k_tiles - 1))
                lp.finc_t(e, mm, 1)
        blk.tensor(t_body)

        def s_body(e):
            for i, (m, n) in enumerate(tiles):
                e.wait_ge(mm, mmb + i + 1)
                evac(m, n, ps[i % 2], e).then_inc(ev, 1)
        blk.scalar(s_body)
    lp.bump("MMm", len(tiles))
    lp.bump("MMe", len(tiles))


def ew_stage(lp, **bodies):
    with lp.block() as blk:
        for eng, fn in bodies.items():
            getattr(blk, eng)(fn)



def norm_chunked(lp, *, x_of, out_emit, scr, kind="ln", eps=1e-5, kd=KD,
                 t_len=T, chunk=512, name="ln"):
    xb, sq = scr["xb"], scr["sq"]
    ones_col, ones_row = scr["ones_col"], scr["ones_row"]
    P0, P1, PS = scr["P0"], scr["P1"], scr["PS"]
    PS2 = P1[0:1, :]
    nch = t_len // chunk
    s = lp.sem(name)
    S1 = scr["S1"][:, 0:chunk]
    S2 = scr["S2"][:, 0:chunk]
    a_r = scr["AR"][:, 0:chunk]
    b_r = scr["BR"][:, 0:chunk]
    m2 = scr["M2"][:, 0:chunk]

    for n in range(nch):
        sl = slice(n * chunk, (n + 1) * chunk)
        with lp.block() as blk:
            def v1(e, sl=sl):
                for k in range(kd):
                    e.tensor_copy(xb[:, k * chunk:(k + 1) * chunk], x_of(k)[:, sl])
                for k in range(kd):
                    e.tensor_tensor(sq[:, k * chunk:(k + 1) * chunk],
                                    xb[:, k * chunk:(k + 1) * chunk],
                                    xb[:, k * chunk:(k + 1) * chunk], ALU.mult)
            blk.vector(v1)

        s0 = lp.cnt[name]
        with lp.block() as blk:
            def t1(e, s0=s0):
                for k in range(kd):
                    e.matmul(PS[:, 0:chunk], ones_col,
                             xb[:, k * chunk:(k + 1) * chunk],
                             start=(k == 0), stop=(k == kd - 1))
                for k in range(kd):
                    e.matmul(PS2[:, 0:chunk], ones_col,
                             sq[:, k * chunk:(k + 1) * chunk],
                             start=(k == 0), stop=(k == kd - 1))
                lp.finc_t(e, s, 1)
            blk.tensor(t1)

            def v2(e, s0=s0):
                e.wait_ge(s, s0 + 1)
                e.tensor_scalar(S1, PS[:, 0:chunk], 1.0 / D, None, ALU.mult)
                e.tensor_scalar(S2, PS2[:, 0:chunk], 1.0 / D, None, ALU.mult)
                if kind == "ln":
                    lp.pad_v(e)
                    e.tensor_tensor(m2, S1, S1, ALU.mult)
                    lp.pad_v(e)
                    e.tensor_tensor(S2, S2, m2, ALU.subtract)
            blk.vector(v2)
        lp.bump(name, 1)

        with lp.block() as blk:
            def sc1(e):
                e.activation(m2, S2, AF.Ln, bias=float(eps))
                lp.pad_s(e)
                e.activation(a_r, m2, AF.Exp, scale=-0.5)
            blk.scalar(sc1)

        with lp.block() as blk:
            def v4(e):
                if kind == "ln":
                    e.tensor_tensor(m2, S1, a_r, ALU.mult)
                    lp.pad_v(e)
                    e.tensor_scalar(b_r, m2, -1.0, None, ALU.mult)
                else:
                    e.tensor_scalar(b_r, a_r, 0.0, None, ALU.mult)
            blk.vector(v4)

        s1c = lp.cnt[name]
        with lp.block() as blk:
            def t3(e, s1c=s1c):
                e.matmul(P0[:, 0:chunk], ones_row, a_r, start=True, stop=True)
                e.matmul(P1[:, 0:chunk], ones_row, b_r, start=True, stop=True)
                lp.finc_t(e, s, 1)
            blk.tensor(t3)

            def v5(e, n=n, sl=sl, s1c=s1c):
                e.wait_ge(s, s1c + 1)
                for k in range(kd):
                    out_emit(k, n, sl, e)
            blk.vector(v5)
        lp.bump(name, 1)



def emit_quant_fp4(lp, e, q, out, W):
    nb = W // 16
    ax, acc = q["ax"][:, 0:W], q["acc"][:, 0:W]
    x = q["t"][:, 0:W]
    bm, sc, rs = q["bm"][:, 0:nb], q["sc"][:, 0:nb], q["rs"][:, 0:nb]
    x3 = x.rearrange("p (b s) -> p b s", s=16)
    sc2 = q["sc2"]
    e.tensor_scalar(ax, x, -1.0, None, ALU.mult)
    e.tensor_tensor(ax, ax, x, ALU.max)
    a3 = ax.rearrange("p (b s) -> p b s", s=16)
    l1 = sc2[:, 0:nb * 8].rearrange("p (b s) -> p b s", s=8)
    l2 = sc2[:, nb * 8:nb * 12].rearrange("p (b s) -> p b s", s=4)
    l3 = sc2[:, nb * 12:nb * 14].rearrange("p (b s) -> p b s", s=2)
    e.tensor_tensor(l1, a3[:, :, 0:8], a3[:, :, 8:16], ALU.max)
    e.tensor_tensor(l2, l1[:, :, 0:4], l1[:, :, 4:8], ALU.max)
    e.tensor_tensor(l3, l2[:, :, 0:2], l2[:, :, 2:4], ALU.max)
    e.tensor_tensor(bm.rearrange("p (b s) -> p b s", s=1),
                    l3[:, :, 0:1], l3[:, :, 1:2], ALU.max)
    e.tensor_scalar(sc, bm, 1.0 / 6.0, 1e-12, ALU.mult, ALU.max)
    lp.pad_v(e)
    e.reciprocal(rs, sc)
    lp.pad_v(e)
    lp.pad_v(e)
    rs_b = rs.rearrange("p (b o) -> p b o", o=1).to_broadcast([128, nb, 16])
    sc_b = sc.rearrange("p (b o) -> p b o", o=1).to_broadcast([128, nb, 16])
    e.tensor_tensor(x3, x3, rs_b, ALU.mult)
    e.tensor_scalar(ax, x, -1.0, None, ALU.mult)
    e.tensor_tensor(ax, ax, x, ALU.max)
    e.tensor_scalar(acc, ax, _FP4_MIDS[0], _FP4_INCS[0], ALU.is_gt, ALU.mult)
    for i in range(1, 7):
        e.tensor_scalar(q["sc2"][:, 0:W], ax, _FP4_MIDS[i], _FP4_INCS[i],
                        ALU.is_gt, ALU.mult)
        e.tensor_tensor(acc, acc, q["sc2"][:, 0:W], ALU.add)
    e.tensor_scalar(x, x, 0.0, None, ALU.is_ge)
    e.tensor_scalar(x, x, 2.0, -1.0, ALU.mult, ALU.add)
    e.tensor_tensor(x3, x3, sc_b, ALU.mult)
    return e.tensor_tensor(out[:, 0:W], acc, x, ALU.mult)



def build_launch_A(upto=99, skip_lnp1=False, skip_lnp0=False):
    lp = LP("A")
    xT = lp.dram_in("xT", [128, KD * T], F32)
    xprevT = lp.dram_in("xprevT", [128, KD * 8], F32)
    pmask = lp.dram_in("pmask", [128, 1], F32)
    gam = lp.dram_in("gam", [128, KD], F32)
    bet = lp.dram_in("bet", [128, KD], F32)
    in_w = lp.dram_in("in_w", [128, KD * 4096], F32)
    in_b = lp.dram_in("in_b", [128, 32], F32)
    dw_w = lp.dram_in("dw_w", [128, 16 * 7], F32)
    dw_b = lp.dram_in("dw_b", [128, 16], F32)
    logdec = lp.dram_in("logdec", [128, 16], F32)
    wg_sh = lp.dram_in("wg_sh", [128, 3 * 1024], F32)
    wu_sh = lp.dram_in("wu_sh", [128, 3 * 1024], F32)
    wd_sh = lp.dram_in("wd_sh", [128, 2816], F32)
    ident = lp.dram_in("ident", [128, 128], F32)
    iota_s = lp.dram_in("iota_s", [128, 16], F32)
    dmask = lp.dram_in("dmask", [128, 128], F32)

    o_wgq = lp.dram_out("wgq", [128, 3 * 1024], BF16)
    o_wuq = lp.dram_out("wuq", [128, 3 * 1024], BF16)
    o_wdq = lp.dram_out("wdq", [128, 2816], BF16)
    o_xq = lp.dram_out("xq_tm", [16 * 128, 1024], BF16)
    o_xn = lp.dram_out("xn_bf", [128, KD * T], BF16)
    o_gs = lp.dram_out("gsig", [128, 16 * T], BF16)
    o_cv = lp.dram_out("conv", [128, 16 * T], BF16)
    o_yl = lp.dram_out("yloc", [128, 16 * T], BF16)
    o_E = lp.dram_out("E", [128, 16], F32)
    o_dbg = lp.dram_out("dbg", [128, 16 * 8], F32)
    o_dbg2 = lp.dram_out("dbg2", [128, KD * 8], F32)

    lp.sem("st")

    xh = lp.sbuf("xh", [128, KD * T], F32)
    xn2 = lp.sbuf("xn2", [128, KD * T], BF16)
    xb = lp.sbuf("xb", [128, KD * 512], BF16)
    sq = lp.sbuf("sq", [128, KD * 512], BF16)
    sS1 = lp.sbuf("sS1", [1, 512], F32)
    sS2 = lp.sbuf("sS2", [1, 512], F32)
    sAR = lp.sbuf("sAR", [1, 512], F32)
    sBR = lp.sbuf("sBR", [1, 512], F32)
    sM2 = lp.sbuf("sM2", [1, 512], F32)
    ones_c = lp.sbuf("ones_c", [128, 2], BF16)
    ones_r = lp.sbuf("ones_r", [1, 128], F32)
    tsc = lp.sbuf("tsc", [128, 512], F32)
    gbt = lp.sbuf("gbt", [128, 2 * KD], F32)
    inb = lp.sbuf("inb", [128, 32], F32)
    dww = lp.sbuf("dww", [128, 16 * 7], F32)
    dwb = lp.sbuf("dwb", [128, 16], F32)
    ld_ = lp.sbuf("ld_", [128, 16], F32)
    idn = lp.sbuf("idn", [128, 128], F32)
    iot = lp.sbuf("iot", [128, 16], F32)
    dmk = lp.sbuf("dmk", [128, 128], F32)
    pmk = lp.sbuf("pmk", [128, 1], F32)
    xpv = lp.sbuf("xpv", [128, KD * 8], F32)
    vpre = lp.sbuf("vpre", [128, 16 * 8], F32)
    qs = {
        "ax": lp.sbuf("q_ax", [128, 1024], F32),
        "acc": lp.sbuf("q_acc", [128, 1024], F32),
        "t": lp.sbuf("q_t", [128, 1024], F32),
        "sc2": lp.sbuf("q_sc2", [128, 1024], F32),
        "bm": lp.sbuf("q_bm", [128, 64], F32),
        "sc": lp.sbuf("q_sc", [128, 64], F32),
        "rs": lp.sbuf("q_rs", [128, 64], F32),
    }
    qo = lp.sbuf("qo", [128, 1024], BF16)
    xtm = lp.sbuf("xtm", [128, 1024], F32)
    wst = lp.sbuf("wst", [128, 2 * KD * 128], BF16)
    vbuf = lp.sbuf("vbuf", [128, 2056], F32)
    vbbf = lp.sbuf("vbbf", [128, 2056], BF16)
    gvst = lp.sbuf("gvst", [128, T], BF16)
    cvst = lp.sbuf("cvst", [128, T], BF16)
    ylst = lp.sbuf("ylst", [128, T], BF16)
    dtap = lp.sbuf("dtap", [128, 7 * 128], BF16)
    u0 = lp.sbuf("u0", [128, T], F32)
    u1 = lp.sbuf("u1", [128, T], F32)
    yb = lp.sbuf("ybuf", [128, T], F32)
    pw1 = lp.sbuf("pw1", [128, 16], F32)
    pw2 = lp.sbuf("pw2", [128, 16], F32)
    pw3 = lp.sbuf("pw3", [128, 16], F32)
    s_ad = lp.sbuf("s_ad", [128, 1], F32)
    s_lna = lp.sbuf("s_lna", [128, 1], F32)
    s_lna8 = lp.sbuf("s_lna8", [128, 1], F32)
    s_nlna = lp.sbuf("s_nlna", [128, 1], F32)
    s_nlna8 = lp.sbuf("s_nlna8", [128, 1], F32)
    s_d16 = lp.sbuf("s_d16", [128, 1], F32)
    Bsc = lp.sbuf("Bsc", [128, 128], F32)
    Asc = lp.sbuf("Asc", [128, 128], F32)
    Atm = lp.sbuf("Atm", [128, 128], F32)
    Esb = lp.sbuf("Esb", [128, 16], F32)

    P0 = lp.psum("P0", [128, 512])
    P1 = lp.psum("P1", [128, 512])
    P2 = lp.psum("P2", [128, 512])
    PS = P2[0:1, :]

    scr = {"xb": xb, "sq": sq, "ones_col": ones_c[:, 0:1],
           "ones_row": ones_r[:, :], "P0": P0, "P1": P1, "PS": PS,
           "S1": sS1, "S2": sS2, "AR": sAR, "BR": sBR, "M2": sM2}

    lp.setup_consts([1e-5])

    ld1 = dma_in(lp, [
        (xh[:, :], xT), (xpv[:, :], xprevT), (pmk[:, :], pmask),
        (gbt[:, 0:KD], gam), (gbt[:, KD:2 * KD], bet), (inb[:, :], in_b),
        (dww[:, :], dw_w), (dwb[:, :], dw_b), (ld_[:, :], logdec),
        (idn[:, :], ident), (iot[:, :], iota_s), (dmk[:, :], dmask),
    ], engine="sync")

    with lp.block() as blk:
        def vinit(e):
            e.memset(ones_c[:, :], 1.0)
            e.memset(ones_r[:, :], 1.0)
        blk.vector(vinit)

    def emit_quant2(e, q, out, W):
        nb = W // 16
        ax, acc = q["ax"][:, 0:W], q["acc"][:, 0:W]
        x = q["t"][:, 0:W]
        bm, sc, rs = q["bm"][:, 0:nb], q["sc"][:, 0:nb], q["rs"][:, 0:nb]
        x3 = x.rearrange("p (b s) -> p b s", s=16)
        e.tensor_reduce(bm, x3, AX.X, ALU.max, apply_absolute_value=True)
        e.tensor_scalar(sc, bm, 1.0 / 6.0, 1e-12, ALU.mult, ALU.max)
        lp.pad_v(e)
        e.reciprocal(rs, sc)
        lp.pad_v(e)
        lp.pad_v(e)
        rs_b = rs.rearrange("p (b o) -> p b o", o=1).to_broadcast([128, nb, 16])
        sc_b = sc.rearrange("p (b o) -> p b o", o=1).to_broadcast([128, nb, 16])
        e.tensor_tensor(x3, x3, rs_b, ALU.mult)
        e.tensor_scalar(ax, x, -1.0, None, ALU.mult)
        e.tensor_tensor(ax, ax, x, ALU.max)
        e.tensor_scalar(acc, ax, _FP4_MIDS[0], _FP4_INCS[0], ALU.is_gt, ALU.mult)
        for i in range(1, 7):
            e.tensor_scalar(q["sc2"][:, 0:W], ax, _FP4_MIDS[i], _FP4_INCS[i],
                            ALU.is_gt, ALU.mult)
            e.tensor_tensor(acc, acc, q["sc2"][:, 0:W], ALU.add)
        e.tensor_scalar(x, x, 0.0, None, ALU.is_ge)
        e.tensor_scalar(x, x, 2.0, -1.0, ALU.mult, ALU.add)
        e.tensor_tensor(x3, x3, sc_b, ALU.mult)
        return e.tensor_tensor(out[:, 0:W], acc, x, ALU.mult)

    def quant_shard(src_dram, out_dram, width):
        for w0 in range(0, width, 1024):
            wch = min(1024, width - w0)
            t0 = dma_in(lp, [(qs["t"][:, 0:wch], src_dram[:, w0:w0 + wch])],
                        engine="gpsimd")
            stw = lp.cnt["st"]
            with lp.block() as blk:
                def v(e, wch=wch, t0=t0, stw=stw):
                    e.wait_ge(lp.sem("gld"), t0)
                    e.wait_ge(lp.sem("st"), stw)
                    if wch < 1024:
                        e.memset(qs["t"][:, wch:1024], 1.0)
                    emit_quant2(e, qs, qo, 1024)
                blk.vector(v)
            dma_in(lp, [(out_dram[:, w0:w0 + wch], qo[:, 0:wch])],
                   engine="sync", sem="st")

    quant_shard(wg_sh, o_wgq, 3 * 1024)
    quant_shard(wu_sh, o_wuq, 3 * 1024)
    quant_shard(wd_sh, o_wdq, 2816)
    if upto <= 1:
        wait_stage(lp, "sync", "st", lp.cnt["st"]); return lp

    wait_stage(lp, "vector", "ld", ld1)

    def xn_apply(k, n, sl, e):
        e.tensor_tensor(tsc[:, :], xh[:, k * T:(k + 1) * T][:, sl],
                        P0[:, 0:512], ALU.mult)
        e.tensor_tensor(xh[:, k * T:(k + 1) * T][:, sl], tsc[:, :],
                        P1[:, 0:512], ALU.add)

    norm_chunked(lp, x_of=lambda k: xh[:, k * T:(k + 1) * T],
                 out_emit=xn_apply, scr=scr, kind="ln", name="nrm")
    if upto <= 2:
        wait_stage(lp, "sync", "st", lp.cnt["st"]); return lp

    with lp.block() as blk:
        def v(e):
            for k in range(KD):
                e.tensor_copy(xn2[:, k * T:(k + 1) * T], xh[:, k * T:(k + 1) * T])
        blk.vector(v)
    dma_in(lp, [(o_xn, xn2[:, :])], engine="sync", sem="st")
    if upto <= 3:
        wait_stage(lp, "sync", "st", lp.cnt["st"]); return lp

    tq = lp.sem("tq")
    for tt in range(16):
        t0 = lp.cnt["tq"]
        with lp.block() as blk:
            def tb(e, tt=tt, t0=t0):
                for k in range(KD):
                    if k >= 1:
                        e.wait_ge(tq, t0 + 2 * k)
                    e.transpose(P2[:, 0:128],
                                xh[:, k * T + tt * 128: k * T + (tt + 1) * 128],
                                idn[:, :])
                    lp.finc_t(e, tq, 1)
            blk.tensor(tb)

            def vb(e, tt=tt, t0=t0):
                for k in range(KD):
                    e.wait_ge(tq, t0 + 2 * k + 1)
                    e.tensor_copy(xtm[:, k * 128:(k + 1) * 128], P2[:, 0:128])\
                        .then_inc(tq, 1)
            blk.vector(vb)
        lp.bump("tq", 2 * KD)
        stw = lp.cnt["st"]
        with lp.block() as blk:
            def vq(e, stw=stw):
                e.wait_ge(lp.sem("st"), stw)
                e.tensor_copy(qs["t"][:, 0:1024], xtm[:, 0:1024])
                emit_quant2(e, qs, qo, 1024)
            blk.vector(vq)
        dma_in(lp, [(o_xq[tt * 128:(tt + 1) * 128, :], qo[:, 0:1024])],
               engine="sync", sem="st")

    if upto <= 4:
        wait_stage(lp, "sync", "st", lp.cnt["st"]); return lp
    wait_stage(lp, "vector", "st", lp.cnt["st"])

    def xn2_apply(k, n, sl, e):
        e.tensor_tensor(tsc[:, :], xh[:, k * T:(k + 1) * T][:, sl],
                        P0[:, 0:512], ALU.mult)
        e.tensor_tensor(tsc[:, :], tsc[:, :], P1[:, 0:512], ALU.add)
        e.tensor_scalar(xn2[:, k * T:(k + 1) * T][:, sl], tsc[:, :],
                        gbt[:, k:k + 1], gbt[:, KD + k:KD + k + 1],
                        ALU.mult, ALU.add)

    norm_chunked(lp, x_of=lambda k: xh[:, k * T:(k + 1) * T],
                 out_emit=xn2_apply, scr=scr, kind="ln", name="nrm")
    if upto <= 5:
        wait_stage(lp, "sync", "st", lp.cnt["st"]); return lp

    def xpv_apply(k, n, sl, e):
        e.tensor_tensor(tsc[:, 0:8], xpv[:, k * 8:(k + 1) * 8], P0[:, 0:8], ALU.mult)
        lp.pad_v(e)
        e.tensor_tensor(xpv[:, k * 8:(k + 1) * 8], tsc[:, 0:8], P1[:, 0:8], ALU.add)

    if not skip_lnp0:
        norm_chunked(lp, x_of=lambda k: xpv[:, k * 8:(k + 1) * 8],
                     out_emit=xpv_apply, scr=scr, kind="ln", name="nrm",
                     t_len=8, chunk=8)

    if skip_lnp1:
        dma_in(lp, [(o_dbg2, xpv[:, :])], engine="sync", sem="st")
        wait_stage(lp, "sync", "st", lp.cnt["st"])
        return lp

    def xpv2_apply(k, n, sl, e):
        e.tensor_tensor(tsc[:, 0:8], xpv[:, k * 8:(k + 1) * 8], P0[:, 0:8], ALU.mult)
        lp.pad_v(e)
        e.tensor_tensor(tsc[:, 0:8], tsc[:, 0:8], P1[:, 0:8], ALU.add)
        lp.pad_v(e)
        e.tensor_scalar(xpv[:, k * 8:(k + 1) * 8], tsc[:, 0:8],
                        gbt[:, k:k + 1], gbt[:, KD + k:KD + k + 1],
                        ALU.mult, ALU.add)

    norm_chunked(lp, x_of=lambda k: xpv[:, k * 8:(k + 1) * 8],
                 out_emit=xpv2_apply, scr=scr, kind="ln", name="nrm",
                 t_len=8, chunk=8)

    with lp.block() as blk:
        def v(e):
            for k in range(KD):
                e.tensor_copy(xb[:, k * 8:(k + 1) * 8], xpv[:, k * 8:(k + 1) * 8])
        blk.vector(v)

    if upto <= 6:
        wait_stage(lp, "sync", "st", lp.cnt["st"]); return lp
    pvs = lp.sem("pv")
    for g in range(8):
        lw = dma_in(lp, [
            (wst[:, 0:KD * 256],
             in_w.rearrange("p (k m) -> p k m", m=4096)[:, :, (16 + 2 * g) * 128:(16 + 2 * g + 2) * 128]),
        ], engine="gpsimd")
        base = lp.cnt["pv"]
        with lp.block() as blk:
            def tb(e, lw=lw, base=base):
                e.wait_ge(lp.sem("gld"), lw)
                for j, pdst in ((0, P0), (1, P2)):
                    for k in range(KD):
                        e.matmul(pdst[:, 0:8],
                                 wst[:, k * 256 + j * 128: k * 256 + (j + 1) * 128],
                                 xb[:, k * 8:(k + 1) * 8],
                                 start=(k == 0), stop=(k == KD - 1))
                lp.finc_t(e, pvs, 1)
            blk.tensor(tb)

            def sv(e, g=g, base=base):
                e.wait_ge(pvs, base + 1)
                for j, psrc in ((0, P0), (1, P2)):
                    mt = 2 * g + j
                    e.activation(tsc[:, j * 8:(j + 1) * 8], psrc[:, 0:8],
                                 AF.Identity, bias=inb[:, 16 + mt:17 + mt])
            blk.scalar(sv)
        lp.bump("pv", 1)

        with lp.block() as blk:
            def vv(e, g=g):
                for j in range(2):
                    mt = 2 * g + j
                    e.tensor_scalar(vpre[:, mt * 8:(mt + 1) * 8],
                                    tsc[:, j * 8:(j + 1) * 8],
                                    pmk[:, 0:1], None, ALU.mult)
            blk.vector(vv)

    if upto <= 7:
        wait_stage(lp, "sync", "st", lp.cnt["st"]); return lp
    for ct in range(16):
        if upto <= 8 + ct:
            break
        if ct > 0:
            with lp.block() as blk:
                def w1(e):
                    e.wait_ge(lp.sem("st"), lp.cnt["st"])
                blk.scalar(w1)

                def w2(e):
                    e.wait_ge(lp.sem("st"), lp.cnt["st"])
                blk.vector(w2)
        lw = dma_in(lp, [
            (wst[:, 0:KD * 128],
             in_w.rearrange("p (k m) -> p k m", m=4096)[:, :, ct * 128:(ct + 1) * 128]),
            (wst[:, KD * 128:2 * KD * 128],
             in_w.rearrange("p (k m) -> p k m", m=4096)[:, :, (16 + ct) * 128:(16 + ct + 1) * 128]),
        ], engine="gpsimd")
        wait_stage(lp, "tensor", "gld", lw)

        def g_evac(m, n, ps, e, ct=ct):
            return e.activation(gvst[:, n * 512:(n + 1) * 512],
                                ps[:, :], AF.Sigmoid, bias=inb[:, ct:ct + 1])

        mm_stage(lp, name=f"ga{ct}",
                 lhsT=lambda k, m: wst[:, k * 128:(k + 1) * 128],
                 rhs=lambda k, n: xn2[:, k * T + n * 512: k * T + (n + 1) * 512],
                 evac=g_evac, tiles=[(0, n) for n in range(4)],
                 k_tiles=KD, psA=P0, psB=P1)

        def v_evac(m, n, ps, e, ct=ct):
            return e.activation(vbuf[:, 8 + n * 512: 8 + (n + 1) * 512],
                                ps[:, :], AF.Identity, bias=inb[:, 16 + ct:17 + ct])

        mm_stage(lp, name=f"vv{ct}",
                 lhsT=lambda k, m: wst[:, KD * 128 + k * 128: KD * 128 + (k + 1) * 128],
                 rhs=lambda k, n: xn2[:, k * T + n * 512: k * T + (n + 1) * 512],
                 evac=v_evac, tiles=[(0, n) for n in range(4)],
                 k_tiles=KD, psA=P0, psB=P1)

        with lp.block() as blk:
            def v(e, ct=ct):
                e.tensor_copy(vbuf[:, 0:8], vpre[:, ct * 8:(ct + 1) * 8])
                lp.pad_v(e)
                e.tensor_copy(vbbf[:, :], vbuf[:, :])
                for tp in range(7):
                    e.tensor_scalar(dtap[:, tp * 128:(tp + 1) * 128],
                                    dmk[:, :], dww[:, ct * 7 + tp:ct * 7 + tp + 1],
                                    None, ALU.mult)
            blk.vector(v)

        def c_evac(m, n, ps, e, ct=ct):
            return e.activation(cvst[:, n * 512:(n + 1) * 512],
                                ps[:, :], AF.Identity, bias=dwb[:, ct:ct + 1])

        mm_stage(lp, name=f"cv{ct}",
                 lhsT=lambda k, m: dtap[:, k * 128:(k + 1) * 128],
                 rhs=lambda k, n: vbbf[:, 2 + n * 512 + k: 2 + n * 512 + k + 512],
                 evac=c_evac, tiles=[(0, n) for n in range(4)],
                 k_tiles=7, psA=P0, psB=P1)

        vown = vbuf[:, 8:2056]
        with lp.block() as blk:
            def sc_b(e, ct=ct):
                e.activation(s_ad[:, :], ld_[:, ct:ct + 1], AF.Sigmoid)
                lp.pad_s(e)
                e.activation(s_lna[:, :], s_ad[:, :], AF.Ln)
            blk.scalar(sc_b)
        with lp.block() as blk:
            def v_b(e):
                e.tensor_scalar(s_lna8[:, :], s_lna[:, :], 8.0, None, ALU.mult)
                e.tensor_scalar(s_nlna[:, :], s_lna[:, :], -1.0, None, ALU.mult)
                lp.pad_v(e)
                e.tensor_scalar(s_nlna8[:, :], s_lna8[:, :], -1.0, None, ALU.mult)
            blk.vector(v_b)
        with lp.block() as blk:
            def sc_c(e):
                e.activation(pw1[:, :], iot[:, :], AF.Exp,
                             scale=s_nlna[:, :], bias=s_lna8[:, :])
                e.activation(pw2[:, :], iot[:, :], AF.Exp,
                             scale=s_lna[:, :], bias=s_nlna8[:, :])
                e.activation(pw3[:, :], iot[:, :], AF.Exp,
                             scale=s_lna[:, :], bias=s_lna[:, :])
                e.activation(s_d16[:, :], s_lna[:, :], AF.Exp, scale=16.0)
            blk.scalar(sc_c)
        with lp.block() as blk:
            def v_c(e, ct=ct, vown=vown):
                pw1b = pw1[:, :].rearrange("p (o s) -> p o s", o=1)\
                    .to_broadcast([128, 128, 16])
                pw2b = pw2[:, :].rearrange("p (o s) -> p o s", o=1)\
                    .to_broadcast([128, 128, 16])
                pw3b = pw3[:, :].rearrange("p (o s) -> p o s", o=1)\
                    .to_broadcast([128, 127, 16])
                v3 = vown.rearrange("p (b s) -> p b s", s=16)
                e.tensor_tensor(u0[:, :].rearrange("p (b s) -> p b s", s=16),
                                v3, pw1b, ALU.mult)
                src_, dst_ = u0, u1
                for kk in (1, 2, 4, 8):
                    s3 = src_[:, :].rearrange("p (b s) -> p b s", s=16)
                    d3 = dst_[:, :].rearrange("p (b s) -> p b s", s=16)
                    e.tensor_tensor(d3[:, :, kk:16], s3[:, :, kk:16],
                                    s3[:, :, 0:16 - kk], ALU.add)
                    e.tensor_copy(d3[:, :, 0:kk], s3[:, :, 0:kk])
                    src_, dst_ = dst_, src_
                e.tensor_tensor(yb[:, :].rearrange("p (b s) -> p b s", s=16),
                                src_[:, :].rearrange("p (b s) -> p b s", s=16),
                                pw2b, ALU.mult)
                e.tensor_copy(Bsc[:, :],
                              yb[:, :].rearrange("p (b s) -> p b s", s=16)[:, :, 15:16]
                              .rearrange("p b o -> p (b o)"))
                e.tensor_copy(Asc[:, :], s_d16[:, :].to_broadcast([128, 128]))
                for kk in (1, 2, 4, 8, 16, 32, 64):
                    e.tensor_tensor(Atm[:, 0:128 - kk], Asc[:, kk:128],
                                    Bsc[:, 0:128 - kk], ALU.mult)
                    e.tensor_tensor(Bsc[:, kk:128], Bsc[:, kk:128],
                                    Atm[:, 0:128 - kk], ALU.add)
                    e.tensor_copy(Atm[:, 0:128 - kk], Asc[:, 0:128 - kk])
                    e.tensor_tensor(Asc[:, kk:128], Asc[:, kk:128],
                                    Atm[:, 0:128 - kk], ALU.mult)
                bscb = Bsc[:, 0:127].rearrange("p (b o) -> p b o", o=1)\
                    .to_broadcast([128, 127, 16])
                u1c = u1[:, 16:T].rearrange("p (b s) -> p b s", s=16)
                e.tensor_tensor(u1c, bscb, pw3b, ALU.mult)
                e.tensor_tensor(yb[:, 16:T], yb[:, 16:T], u1[:, 16:T], ALU.add)
                e.tensor_copy(Esb[:, ct:ct + 1], Bsc[:, 127:128])
                e.tensor_copy(ylst[:, :], yb[:, :])
            blk.vector(v_c)

        dma_in(lp, [
            (o_gs[:, ct * T:(ct + 1) * T], gvst[:, :]),
            (o_cv[:, ct * T:(ct + 1) * T], cvst[:, :]),
            (o_yl[:, ct * T:(ct + 1) * T], ylst[:, :]),
        ], engine="sync", sem="st")

    dma_in(lp, [(o_E, Esb[:, :]), (o_dbg, vpre[:, :]), (o_dbg2, xpv[:, :])], engine="sync", sem="st")
    wait_stage(lp, "sync", "st", lp.cnt["st"])
    return lp



def _consts_A():
    ident = np.eye(128, dtype=np.float32)
    iota_s = np.broadcast_to(np.arange(16, dtype=np.float32), (128, 16)).copy()
    dmask = np.eye(128, dtype=np.float32)
    return ident, iota_s, dmask


def prep_A(inp):
    x = _f32(inp["L0_attn_x"])
    ident, iota_s, dmask = _consts_A()
    gam = _pcol(inp["L0_attn_norm_gamma"])
    bet = _pcol(inp["L0_attn_norm_beta"])
    in_w = _wlay(inp["L0_attn_in_w"])
    in_b = _pcol(inp["L0_attn_in_b"])
    dw = _f32(inp["L0_attn_dw_w"])
    dw_w = np.ascontiguousarray(dw.reshape(16, 128, 7).transpose(1, 0, 2).reshape(128, 112))
    dw_b = _pcol(inp["L0_attn_dw_b"])
    logdec = _pcol(inp["L0_attn_log_decay"])
    wg = _f32(inp["L0_ffn_w_gate"])
    wu = _f32(inp["L0_ffn_w_up"])
    wd = _f32(inp["L0_ffn_w_down"])
    maps = []
    for c in range(NCORES):
        b, r = c // 4, c % 4
        xs = _core_slice(x, c)
        if r == 0:
            xprev = np.zeros((8, D), np.float32)
            pmask = np.zeros((128, 1), np.float32)
        else:
            xprev = x[b, r * T - 8: r * T, :]
            pmask = np.ones((128, 1), np.float32)
        wg_sh = np.concatenate([wg[c * 352:(c + 1) * 352], np.ones((32, 1024), np.float32)], 0)
        wg_sh = np.ascontiguousarray(wg_sh.reshape(3, 128, 1024).transpose(1, 0, 2).reshape(128, 3072))
        wu_sh = np.concatenate([wu[c * 352:(c + 1) * 352], np.ones((32, 1024), np.float32)], 0)
        wu_sh = np.ascontiguousarray(wu_sh.reshape(3, 128, 1024).transpose(1, 0, 2).reshape(128, 3072))
        wd_sh = np.ascontiguousarray(wd[c * 128:(c + 1) * 128])
        maps.append(dict(
            xT=_fm(xs), xprevT=_fm(xprev), pmask=pmask, gam=gam, bet=bet,
            in_w=in_w, in_b=in_b, dw_w=dw_w, dw_b=dw_b, logdec=logdec,
            wg_sh=wg_sh, wu_sh=wu_sh, wd_sh=wd_sh,
            ident=ident, iota_s=iota_s, dmask=dmask,
        ))
    return maps


def unshard_quant(shards, rows):
    out = []
    for sh in shards:
        a = np.asarray(sh).reshape(128, 3, 1024).transpose(1, 0, 2).reshape(384, 1024)
        out.append(a[:352])
    return np.concatenate(out, 0)[:rows]



def build_launch_B():
    lp = LP("B")
    xT = lp.dram_in("xT", [128, KD * T], F32)
    xnb = lp.dram_in("xnb", [128, KD * T], BF16)
    yloc = lp.dram_in("yloc", [128, 16 * T], BF16)
    conv = lp.dram_in("conv", [128, 16 * T], BF16)
    gsig = lp.dram_in("gsig", [128, 16 * T], BF16)
    Eprev = lp.dram_in("Eprev", [128, 48], F32)
    logdec = lp.dram_in("logdec", [128, 16], F32)
    iota_t1 = lp.dram_in("iota_t1", [128, T], F32)
    mix_w = lp.dram_in("mix_w", [128, 16 * 2048], F32)
    mix_b = lp.dram_in("mix_b", [128, 16], F32)
    out_w = lp.dram_in("out_w", [128, 16 * 1024], F32)
    out_b = lp.dram_in("out_b", [128, 8], F32)
    xq_fm = lp.dram_in("xq_fm", [128, KD * T], BF16)
    wgq_r = lp.dram_in("wgq_r", [128, KD * 2816], BF16)
    wuq_r = lp.dram_in("wuq_r", [128, KD * 2816], BF16)
    wdq_l = lp.dram_in("wdq_l", [128, 22 * 1024], BF16)
    w_lqkv = lp.dram_in("w_lqkv", [128, 4 * 1536], F32)
    b_lqkv = lp.dram_in("b_lqkv", [128, 12], F32)
    w_comp = lp.dram_in("w_comp", [128, 32 * 1024], F32)
    b_comp = lp.dram_in("b_comp", [128, 8], F32)
    w_gqkv = lp.dram_in("w_gqkv", [128, 4 * 1536], F32)
    b_gqkv = lp.dram_in("b_gqkv", [128, 12], F32)
    ident = lp.dram_in("ident", [128, 128], F32)

    o_x1 = lp.dram_out("x1", [128, KD * T], F32)
    o_xn1 = lp.dram_out("xn1", [128, KD * T], BF16)
    o_lqkv = lp.dram_out("lqkv", [128, 12 * T], BF16)
    o_gqkv = lp.dram_out("gqkv", [128, 12 * 512], BF16)
    o_s0 = lp.dram_out("s0d", [128, KD * T], F32)
    o_g0 = lp.dram_out("g0d", [128, KD * T], BF16)
    o_htm = lp.dram_out("htmd", [128, 2 * 2816], F32)
    o_hq2 = lp.dram_out("hq2d", [128, 2816], BF16)
    o_hqf = lp.dram_out("hqfd", [128, 22 * 512], BF16)
    h_scr = lp.dram_scratch("h_scr", [128, 16 * T], BF16)

    lp.sem("st"); lp.sem("ld"); lp.sem("gld")
    lp.setup_consts([1e-5])

    biasb = lp.sbuf("biasb", [128, 64], F32)
    ld16 = lp.sbuf("ld16", [128, 16], F32)
    ad16 = lp.sbuf("ad16", [128, 16], F32)
    lna16 = lp.sbuf("lna16", [128, 16], F32)
    d2048 = lp.sbuf("d2048", [128, 16], F32)
    Ep = lp.sbuf("Ep", [128, 48], F32)
    C16 = lp.sbuf("C16", [128, 16], F32)
    Ct1 = lp.sbuf("Ct1", [128, 16], F32)
    idnb = lp.sbuf("idnb", [128, 128], BF16)
    idnf = lp.sbuf("idnf", [128, 128], F32)
    wlq = lp.sbuf("wlq", [128, 4 * 1536], BF16)
    wgq2 = lp.sbuf("wgq2", [128, 4 * 1536], BF16)
    xb = lp.sbuf("xb", [128, KD * 512], BF16)
    sq = lp.sbuf("sq", [128, KD * 512], BF16)
    sS1 = lp.sbuf("sS1", [1, 512], F32)
    sS2 = lp.sbuf("sS2", [1, 512], F32)
    sAR = lp.sbuf("sAR", [1, 512], F32)
    sBR = lp.sbuf("sBR", [1, 512], F32)
    sM2 = lp.sbuf("sM2", [1, 512], F32)
    ones_c = lp.sbuf("ones_c", [128, 2], BF16)
    ones_r = lp.sbuf("ones_r", [1, 128], F32)
    tsc = lp.sbuf("tsc", [128, 512], F32)
    qs = {
        "ax": lp.sbuf("q_ax", [128, 1024], F32),
        "acc": lp.sbuf("q_acc", [128, 1024], F32),
        "t": lp.sbuf("q_t", [128, 1024], F32),
        "sc2": lp.sbuf("q_sc2", [128, 1024], F32),
        "bm": lp.sbuf("q_bm", [128, 64], F32),
        "sc": lp.sbuf("q_sc", [128, 64], F32),
        "rs": lp.sbuf("q_rs", [128, 64], F32),
    }
    qo = lp.sbuf("qo", [128, 1024], BF16)

    P0 = lp.psum("P0", [128, 512])
    P1 = lp.psum("P1", [128, 512])
    P2 = lp.psum("P2", [128, 512])
    scr = {"xb": xb, "sq": sq, "ones_col": ones_c[:, 0:1],
           "ones_row": ones_r[:, :], "P0": P0, "P1": P1, "PS": P2[0:1, :],
           "S1": sS1, "S2": sS2, "AR": sAR, "BR": sBR, "M2": sM2}

    ld1 = dma_in(lp, [
        (Ep[:, :], Eprev), (ld16[:, :], logdec),
        (biasb[:, 0:16], mix_b), (biasb[:, 16:24], out_b),
        (biasb[:, 24:36], b_lqkv), (biasb[:, 36:44], b_comp),
        (biasb[:, 44:56], b_gqkv), (idnf[:, :], ident),
    ], engine="sync")
    lw1 = dma_in(lp, [(wlq[:, :], w_lqkv), (wgq2[:, :], w_gqkv)], engine="gpsimd")
    with lp.block() as blk:
        def vinit(e, ld1=ld1):
            e.memset(ones_c[:, :], 1.0)
            e.memset(ones_r[:, :], 1.0)
            e.wait_ge(lp.sem("ld"), ld1)
            e.tensor_copy(idnb[:, :], idnf[:, :])
        blk.vector(vinit)
    with lp.block() as blk:
        def s1(e):
            e.activation(ad16[:, :], ld16[:, :], AF.Sigmoid)
            lp.pad_s(e)
            e.activation(lna16[:, :], ad16[:, :], AF.Ln)
            lp.pad_s(e)
            e.activation(d2048[:, :], lna16[:, :], AF.Exp, scale=2048.0)
        blk.scalar(s1)
    with lp.block() as blk:
        def v1(e):
            e.tensor_tensor(Ct1[:, :], d2048[:, :], Ep[:, 32:48], ALU.mult)
            lp.pad_v(e)
            e.tensor_tensor(C16[:, :], Ct1[:, :], Ep[:, 16:32], ALU.add)
            lp.pad_v(e)
            e.tensor_tensor(Ct1[:, :], d2048[:, :], C16[:, :], ALU.mult)
            lp.pad_v(e)
            e.tensor_tensor(C16[:, :], Ct1[:, :], Ep[:, 0:16], ALU.add)
        blk.vector(v1)

    with lp.scope() as sA:
        BB1 = lp.sb(sA, "BB1", [128, 16 * T], BF16)
        ytile = lp.sb(sA, "ytile", [128, T], BF16)
        powt = lp.sb(sA, "powt", [128, T], F32)
        pows = lp.sb(sA, "pows", [128, T], F32)
        iot1 = lp.sb(sA, "iot1", [128, T], F32)
        ctile = lp.sb(sA, "ctile", [128, T], BF16)
        gtile = lp.sb(sA, "gtile", [128, T], BF16)
        mtmp = lp.sb(sA, "mtmp", [128, T], F32)
        hst = lp.sb(sA, "hst", [128, 2 * T], BF16)
        wrow = lp.sb(sA, "wrow", [128, 16 * 128], BF16)
        ld2 = dma_in(lp, [(iot1[:, :], iota_t1)], engine="sync")
        for ct in range(16):
            t0 = dma_in(lp, [(ytile[:, :], yloc[:, ct * T:(ct + 1) * T])],
                        engine="sync")
            with lp.block() as blk:
                def sa(e, ct=ct, t0=t0):
                    e.wait_ge(lp.sem("ld"), t0)
                    e.activation(powt[:, :], iot1[:, :], AF.Exp,
                                 scale=lna16[:, ct:ct + 1])
                blk.scalar(sa)
            with lp.block() as blk:
                def va(e, ct=ct):
                    e.tensor_scalar(pows[:, :], powt[:, :], C16[:, ct:ct + 1],
                                    None, ALU.mult)
                    e.tensor_tensor(BB1[:, ct * T:(ct + 1) * T], ytile[:, :],
                                    pows[:, :], ALU.add)
                blk.vector(va)

        for mt in range(16):
            lw = dma_in(lp, [
                (wrow[:, :],
                 mix_w.rearrange("p (k m) -> p k m", m=2048)[:, :, mt * 128:(mt + 1) * 128]),
                (ctile[:, :], conv[:, mt * T:(mt + 1) * T]),
                (gtile[:, :], gsig[:, mt * T:(mt + 1) * T]),
            ], engine="gpsimd")
            wait_stage(lp, "tensor", "gld", lw)
            stw = lp.cnt["st"]

            def mx_ev(m, n, ps, e, mt=mt):
                return e.activation(mtmp[:, n * 512:(n + 1) * 512], ps[:, :],
                                    AF.Identity, bias=biasb[:, mt:mt + 1])

            mm_stage(lp, name="mm",
                     lhsT=lambda k, m: wrow[:, k * 128:(k + 1) * 128],
                     rhs=lambda k, n: BB1[:, k * T + n * 512:k * T + (n + 1) * 512],
                     evac=mx_ev, tiles=[(0, n) for n in range(4)],
                     k_tiles=16, psA=P0, psB=P1)
            with lp.block() as blk:
                def vb(e, mt=mt, stw=stw):
                    e.wait_ge(lp.sem("st"), stw - 16 if stw >= 16 else 0)
                    e.tensor_tensor(mtmp[:, :], mtmp[:, :], ctile[:, :], ALU.add)
                    e.tensor_tensor(hst[:, (mt % 2) * T:(mt % 2 + 1) * T],
                                    mtmp[:, :], gtile[:, :], ALU.mult)
                blk.vector(vb)
            dma_in(lp, [(h_scr[:, mt * T:(mt + 1) * T],
                         hst[:, (mt % 2) * T:(mt % 2 + 1) * T])],
                   engine="sync", sem="st")

    for n in range(4):
        tsl = slice(n * 512, (n + 1) * 512)
        if n > 0:
            with lp.block() as blk:
                def w1(e):
                    e.wait_ge(lp.sem("st"), lp.cnt["st"])
                blk.vector(w1)
                def w2(e):
                    e.wait_ge(lp.sem("st"), lp.cnt["st"])
                blk.scalar(w2)
        with lp.scope() as sB:
            hsl = lp.sb(sB, "hsl", [128, 16 * 512], BF16)
            xnsl = lp.sb(sB, "xnsl", [128, KD * 512], BF16)
            s0c = lp.sb(sB, "s0c", [128, KD * 512], F32)
            ldh = dma_in(lp, [
                (hsl[:, :], h_scr.rearrange("p (k t) -> p k t", t=T)[:, :, tsl]),
                (xnsl[:, :], xnb.rearrange("p (k t) -> p k t", t=T)[:, :, tsl]),
            ], engine="sync")
            wait_stage(lp, "tensor", "ld", ldh)
            wait_stage(lp, "vector", "ld", ldh)
            with lp.scope() as sC:
                wrow2 = lp.sb(sC, "wrow2", [128, 16 * 128], BF16)
                for mt in range(KD):
                    lw = dma_in(lp, [
                        (wrow2[:, :],
                         out_w.rearrange("p (k m) -> p k m", m=1024)[:, :, mt * 128:(mt + 1) * 128]),
                    ], engine="gpsimd")
                    wait_stage(lp, "tensor", "gld", lw)

                    def ow_ev(m, nn, ps, e, mt=mt):
                        return e.activation(tsc[:, :], ps[:, :], AF.Identity,
                                            bias=biasb[:, 16 + mt:17 + mt])

                    mm_stage(lp, name="mm",
                             lhsT=lambda k, m: wrow2[:, k * 128:(k + 1) * 128],
                             rhs=lambda k, nn: hsl[:, k * 512:(k + 1) * 512],
                             evac=ow_ev, tiles=[(0, 0)],
                             k_tiles=16, psA=P0, psB=P1)
                    with lp.block() as blk:
                        def vb(e, mt=mt):
                            e.tensor_tensor(s0c[:, mt * 512:(mt + 1) * 512],
                                            tsc[:, :], xnsl[:, mt * 512:(mt + 1) * 512],
                                            ALU.add)
                        blk.vector(vb)

            g0c = lp.sb(sB, "g0c", [128, KD * 512], BF16)
            with lp.scope() as sD:
                xqsl = lp.sb(sD, "xqsl", [128, KD * 512], BF16)
                wgsl = lp.sb(sD, "wgsl", [128, KD * 512], BF16)
                wusl = lp.sb(sD, "wusl", [128, KD * 512], BF16)
                htm4 = lp.sb(sD, "htm4", [128, 2 * 2816], F32)
                t1f = lp.sb(sD, "t1f", [128, 512], F32)
                t2f = lp.sb(sD, "t2f", [128, 512], F32)
                hq = lp.sb(sD, "hq", [128, 2816], BF16)
                hqf = lp.sb(sD, "hqf", [128, 22 * 512], BF16)
                ldq = dma_in(lp, [
                    (xqsl[:, :], xq_fm.rearrange("p (k t) -> p k t", t=T)[:, :, tsl]),
                ], engine="sync")
                wait_stage(lp, "tensor", "ld", ldq)
                cbs = [(i * 512, 512) for i in range(5)] + [(2560, 256)]
                gu = lp.sem("gu")
                for ttp in range(2):
                    for cb0, cbw in cbs:
                        lw = dma_in(lp, [
                            (wgsl[:, 0:KD * cbw],
                             wgq_r.rearrange("p (k m) -> p k m", m=2816)[:, :, cb0:cb0 + cbw]),
                            (wusl[:, 0:KD * cbw],
                             wuq_r.rearrange("p (k m) -> p k m", m=2816)[:, :, cb0:cb0 + cbw]),
                        ], engine="gpsimd")
                        wait_stage(lp, "tensor", "gld", lw)
                        for tt in (2 * ttp, 2 * ttp + 1):
                            base = lp.cnt["gu"]
                            with lp.block() as blk:
                                def tb(e, tt=tt, cb0=cb0, cbw=cbw, base=base):
                                    for k in range(KD):
                                        e.matmul(P0[:, 0:cbw],
                                                 xqsl[:, k * 512 + tt * 128:k * 512 + (tt + 1) * 128],
                                                 wgsl[:, k * cbw:(k + 1) * cbw],
                                                 start=(k == 0), stop=(k == KD - 1))
                                    for k in range(KD):
                                        e.matmul(P1[:, 0:cbw],
                                                 xqsl[:, k * 512 + tt * 128:k * 512 + (tt + 1) * 128],
                                                 wusl[:, k * cbw:(k + 1) * cbw],
                                                 start=(k == 0), stop=(k == KD - 1))
                                    lp.finc_t(e, gu, 1)
                                blk.tensor(tb)

                                def sb_(e, tt=tt, cb0=cb0, cbw=cbw, base=base):
                                    e.wait_ge(gu, base + 1)
                                    e.activation(t1f[:, 0:cbw], P0[:, 0:cbw], AF.Silu)
                                    lp.finc_s(e, gu, 1)
                                blk.scalar(sb_)

                                def vb(e, tt=tt, cb0=cb0, cbw=cbw, base=base):
                                    e.wait_ge(gu, base + 2)
                                    e.tensor_copy(t2f[:, 0:cbw], P1[:, 0:cbw])
                                    lp.pad_v(e)
                                    e.tensor_tensor(htm4[:, (tt % 2) * 2816 + cb0:(tt % 2) * 2816 + cb0 + cbw],
                                                    t1f[:, 0:cbw], t2f[:, 0:cbw], ALU.mult)
                                blk.vector(vb)
                            lp.bump("gu", 2)
                    tq = lp.sem("tq2")
                    for tt in (2 * ttp, 2 * ttp + 1):
                        with lp.block() as blk:
                            def vq(e, tt=tt):
                                for w0, wch in ((0, 1024), (1024, 1024), (2048, 768)):
                                    e.tensor_copy(qs["t"][:, 0:wch],
                                                  htm4[:, (tt % 2) * 2816 + w0:(tt % 2) * 2816 + w0 + wch])
                                    if wch < 1024:
                                        e.memset(qs["t"][:, wch:1024], 1.0)
                                    emit_quant_fp4(lp, e, qs, qo, 1024)
                                    e.tensor_copy(hq[:, w0:w0 + wch], qo[:, 0:wch])
                            blk.vector(vq)
                        base = lp.cnt["tq2"]
                        P2b = P2[:, :].bitcast(BF16)
                        with lp.block() as blk:
                            def tb(e, tt=tt, base=base, P2b=P2b):
                                for cc in range(22):
                                    if cc >= 1:
                                        e.wait_ge(lp.sem("tq2"), base + 2 * cc)
                                    e.transpose(P2b[:, 0:128], hq[:, cc * 128:(cc + 1) * 128], idnb[:, :])
                                    lp.finc_t(e, lp.sem("tq2"), 1)
                            blk.tensor(tb)

                            def vb(e, tt=tt, base=base, P2b=P2b):
                                for cc in range(22):
                                    e.wait_ge(lp.sem("tq2"), base + 2 * cc + 1)
                                    e.tensor_copy(hqf[:, cc * 512 + tt * 128:cc * 512 + (tt + 1) * 128],
                                                  P2b[:, 0:128])
                                    lp.finc_v(e, lp.sem("tq2"), 1)
                            blk.vector(vb)
                        lp.bump("tq2", 44)
                if n == 0:
                    dma_in(lp, [(o_htm, htm4[:, :]), (o_hqf, hqf[:, :]),
                                (o_hq2, hq[:, :])],
                           engine="sync", sem="st")
                with lp.scope() as sE:
                    wdsl = lp.sb(sE, "wdsl", [128, 22 * 128], BF16)
                    for mt in range(KD):
                        lw = dma_in(lp, [
                            (wdsl[:, :],
                             wdq_l.rearrange("p (k m) -> p k m", m=1024)[:, :, mt * 128:(mt + 1) * 128]),
                        ], engine="gpsimd")
                        wait_stage(lp, "tensor", "gld", lw)

                        def dn_ev(m, nn, ps, e, mt=mt):
                            return e.activation(g0c[:, mt * 512:(mt + 1) * 512],
                                                ps[:, :], AF.Silu)

                        mm_stage(lp, name="mm",
                                 lhsT=lambda k, m: wdsl[:, k * 128:(k + 1) * 128],
                                 rhs=lambda k, nn: hqf[:, k * 512:(k + 1) * 512],
                                 evac=dn_ev, tiles=[(0, 0)],
                                 k_tiles=22, psA=P0, psB=P1)

            x1c = lp.sb(sB, "x1c", [128, KD * 512], F32)
            xn1c = lp.sb(sB, "xn1c", [128, KD * 512], BF16)
            ldx = dma_in(lp, [
                (x1c[:, :], xT.rearrange("p (k t) -> p k t", t=T)[:, :, tsl]),
            ], engine="sync")
            wait_stage(lp, "vector", "ld", ldx)
            with lp.block() as blk:
                def vx(e):
                    for k in range(KD):
                        e.tensor_tensor(tsc[:, :], s0c[:, k * 512:(k + 1) * 512],
                                        g0c[:, k * 512:(k + 1) * 512], ALU.mult)
                        lp.pad_v(e)
                        e.tensor_tensor(x1c[:, k * 512:(k + 1) * 512],
                                        x1c[:, k * 512:(k + 1) * 512], tsc[:, :],
                                        ALU.add)
                blk.vector(vx)
            dma_in(lp, [(o_x1.rearrange("p (k t) -> p k t", t=T)[:, :, tsl], x1c[:, :]),
                        (o_s0.rearrange("p (k t) -> p k t", t=T)[:, :, tsl], s0c[:, :]),
                        (o_g0.rearrange("p (k t) -> p k t", t=T)[:, :, tsl], g0c[:, :])],
                   engine="sync", sem="st")

            def xn1_apply(k, nn, sl2, e):
                e.tensor_tensor(tsc[:, :], x1c[:, k * 512:(k + 1) * 512],
                                P0[:, 0:512], ALU.mult)
                e.tensor_tensor(xn1c[:, k * 512:(k + 1) * 512], tsc[:, :],
                                P1[:, 0:512], ALU.add)

            norm_chunked(lp, x_of=lambda k: x1c[:, k * 512:(k + 1) * 512],
                         out_emit=xn1_apply, scr=scr, kind="ln", name="nrm",
                         t_len=512, chunk=512)
            dma_in(lp, [(o_xn1.rearrange("p (k t) -> p k t", t=T)[:, :, tsl], xn1c[:, :])],
                   engine="sync", sem="st")

            lqst = lp.sb(sB, "lqst", [128, 12 * 512], BF16)

            def lq_ev(m, nn, ps, e):
                return e.activation(lqst[:, m * 512:(m + 1) * 512], ps[:, :],
                                    AF.Identity, bias=biasb[:, 24 + m:25 + m])

            mm_stage(lp, name="mm",
                     lhsT=lambda k, m: wlq[:, k * 1536 + m * 128:k * 1536 + (m + 1) * 128],
                     rhs=lambda k, nn: xn1c[:, k * 512:(k + 1) * 512],
                     evac=lq_ev, tiles=[(m, 0) for m in range(12)],
                     k_tiles=4, psA=P0, psB=P1)
            dma_in(lp, [(o_lqkv.rearrange("p (m t) -> p m t", t=T)[:, :, tsl], lqst[:, :])],
                   engine="sync", sem="st")

            cgc = lp.sb(sB, "cgc", [128, 4 * 128], BF16)
            gqst = lp.sb(sB, "gqst", [128, 12 * 128], BF16)
            with lp.scope() as sF:
                wcsl = lp.sb(sF, "wcsl", [128, 32 * 128], BF16)
                for mt in range(4):
                    lw = dma_in(lp, [
                        (wcsl[:, :],
                         w_comp.rearrange("p (k m) -> p k m", m=1024)[:, :, 512 + mt * 128:512 + (mt + 1) * 128]),
                    ], engine="gpsimd")
                    wait_stage(lp, "tensor", "gld", lw)

                    def cp_ev(m, nn, ps, e, mt=mt):
                        return e.activation(cgc[:, mt * 128:(mt + 1) * 128],
                                            ps[:, 0:128], AF.Identity,
                                            bias=biasb[:, 40 + mt:41 + mt])

                    def cp_rhs(k, nn):
                        r, kf = k // 8, k % 8
                        return xn1c[:, kf * 512 + r: (kf + 1) * 512: 4]

                    mm_stage(lp, name="mm",
                             lhsT=lambda k, m: wcsl[:, k * 128:(k + 1) * 128],
                             rhs=cp_rhs, evac=cp_ev, tiles=[(0, 0)],
                             k_tiles=32, psA=P0, psB=P1, out_free=128)

            def gq_ev(m, nn, ps, e):
                return e.activation(gqst[:, m * 128:(m + 1) * 128], ps[:, 0:128],
                                    AF.Identity, bias=biasb[:, 44 + m:45 + m])

            mm_stage(lp, name="mm",
                     lhsT=lambda k, m: wgq2[:, k * 1536 + m * 128:k * 1536 + (m + 1) * 128],
                     rhs=lambda k, nn: cgc[:, k * 128:(k + 1) * 128],
                     evac=gq_ev, tiles=[(m, 0) for m in range(12)],
                     k_tiles=4, psA=P0, psB=P1, out_free=128)
            dma_in(lp, [(o_gqkv.rearrange("p (m t) -> p m t", t=512)[:, :, n * 128:(n + 1) * 128],
                         gqst[:, :])], engine="sync", sem="st")

    wait_stage(lp, "sync", "st", lp.cnt["st"])
    return lp


def prep_B(inp, A):
    x = _f32(inp["L0_attn_x"])
    iota_t1 = np.broadcast_to(np.arange(1, T + 1, dtype=np.float32), (128, T)).copy()
    ident = np.eye(128, dtype=np.float32)
    mix_w = _wlay(inp["L0_attn_mix_w"])
    mix_b = _pcol(inp["L0_attn_mix_b"])
    out_w = _wlay(inp["L0_attn_out_w"])
    out_b = _pcol(inp["L0_attn_out_b"])
    logdec = _pcol(inp["L0_attn_log_decay"])
    wgq = unshard_quant([A[c]["wgq"] for c in range(NCORES)], 2816)
    wuq = unshard_quant([A[c]["wuq"] for c in range(NCORES)], 2816)
    wdq = np.concatenate([np.asarray(A[c]["wdq"]) for c in range(NCORES)], 0)
    wgq_r = _wlay(np.ascontiguousarray(np.asarray(wgq, np.float32).T)).astype(BFD)
    wuq_r = _wlay(np.ascontiguousarray(np.asarray(wuq, np.float32).T)).astype(BFD)
    wdq_l = _wlay(np.ascontiguousarray(np.asarray(wdq, np.float32).T)).astype(BFD)
    w_lqkv = _wlay(inp["L1_attn_w_lqkv"])
    b_lqkv = _pcol(inp["L1_attn_b_lqkv"])
    w_comp = _wlay(inp["L1_attn_w_comp"])
    b_comp = _pcol(inp["L1_attn_b_comp"])
    w_gqkv = _wlay(inp["L1_attn_w_gqkv"])
    b_gqkv = _pcol(inp["L1_attn_b_gqkv"])
    maps = []
    for c in range(NCORES):
        b, m = c // 4, c % 4
        Z = np.zeros((128, 16), np.float32)
        Ep = np.concatenate([
            np.asarray(A[4 * b + m - 1]["E"]) if m >= 1 else Z,
            np.asarray(A[4 * b + m - 2]["E"]) if m >= 2 else Z,
            np.asarray(A[4 * b + m - 3]["E"]) if m >= 3 else Z,
        ], 1)
        maps.append(dict(
            xT=_fm(_core_slice(x, c)), xnb=np.asarray(A[c]["xn_bf"]),
            yloc=np.asarray(A[c]["yloc"]), conv=np.asarray(A[c]["conv"]),
            gsig=np.asarray(A[c]["gsig"]), Eprev=np.ascontiguousarray(Ep),
            logdec=logdec, iota_t1=iota_t1, mix_w=mix_w, mix_b=mix_b,
            out_w=out_w, out_b=out_b,
            xq_fm=_fm(np.asarray(A[c]["xq_tm"], np.float32)).astype(BFD),
            wgq_r=wgq_r, wuq_r=wuq_r, wdq_l=wdq_l,
            w_lqkv=w_lqkv, b_lqkv=b_lqkv, w_comp=w_comp, b_comp=b_comp,
            w_gqkv=w_gqkv, b_gqkv=b_gqkv, ident=ident,
        ))
    return maps



def emit_flash(lp, *, qT, kT, v_tm, out_sb, scale, Tq, dva,
               cmask, ones_r, P0, P1, P2, ptile, otmp, rcp, name="fl",
               clip80=False, ctile=None):
    fmm = lp.sem(name + "m")
    fex = lp.sem(name + "x")
    fxp = lp.sem(name + "p")
    fav = lp.sem(name + "a")
    fep = lp.sem(name + "e")
    nqc = Tq // 512
    dv = dva - 1
    for qc in range(nqc):
        nkt = (qc + 1) * 4
        bmm, bex, bxp, bav = (lp.cnt[name + "m"], lp.cnt[name + "x"],
                              lp.cnt[name + "p"], lp.cnt[name + "a"])
        with lp.block() as blk:
            def tb(e, qc=qc, nkt=nkt, bxp=bxp):
                for kt in range(nkt):
                    if kt > 0:
                        e.wait_ge(fxp, bxp + kt)
                    e.matmul(P0[:, 0:512], kT[:, kt * 128:(kt + 1) * 128],
                             qT[:, qc * 512:(qc + 1) * 512], start=True, stop=True)
                    lp.finc_t(e, fmm, 1)
                    e.wait_ge(fxp, bxp + kt + 1)
                    e.matmul(P1[0:dva, 0:512], v_tm(kt),
                             ptile[:, (kt % 2) * 512:(kt % 2 + 1) * 512],
                             start=(kt == 0), stop=(kt == nkt - 1))
                    lp.finc_t(e, fav, 1)
            blk.tensor(tb)

            def sc_(e, qc=qc, nkt=nkt, bmm=bmm, bav=bav):
                for kt in range(nkt):
                    e.wait_ge(fmm, bmm + kt + 1)
                    if kt >= 2:
                        e.wait_ge(fav, bav + kt - 1)
                    dst = ptile[:, (kt % 2) * 512:(kt % 2 + 1) * 512]
                    if clip80:
                        e.activation(ctile[:, 0:512], P0[:, 0:512], AF.Relu,
                                     bias=80.0)
                        lp.pad_s(e)
                        e.activation(ctile[:, 0:512], ctile[:, 0:512], AF.Relu,
                                     scale=-1.0, bias=160.0)
                        lp.pad_s(e)
                        e.activation(dst, ctile[:, 0:512], AF.Exp, scale=-scale)
                    else:
                        e.activation(dst, P0[:, 0:512], AF.Exp, scale=scale)
                    lp.finc_s(e, fex, 1)
                    if kt < nkt - 4:
                        lp.finc_s(e, fxp, 1)
            blk.scalar(sc_)

            def vb(e, qc=qc, nkt=nkt, bex=bex):
                for kt in range(nkt - 4, nkt):
                    r = kt - (nkt - 4)
                    e.wait_ge(fex, bex + kt + 1)
                    e.tensor_tensor(ptile[:, (kt % 2) * 512:(kt % 2 + 1) * 512],
                                    ptile[:, (kt % 2) * 512:(kt % 2 + 1) * 512],
                                    cmask[:, r * 512:(r + 1) * 512], ALU.mult)
                    lp.finc_v(e, fxp, 1)
            blk.vector(vb)
        lp.bump(name + "m", nkt)
        lp.bump(name + "x", nkt)
        lp.bump(name + "p", nkt)
        lp.bump(name + "a", nkt)

        bep = lp.cnt[name + "e"]
        with lp.block() as blk:
            def sc2(e):
                e.activation(otmp[0:dva, 0:512], P1[0:dva, 0:512], AF.Identity)
                lp.finc_s(e, fep, 1)
            blk.scalar(sc2)

            def vr(e, bep=bep):
                e.wait_ge(fep, bep + 1)
                e.reciprocal(rcp[:, :], otmp[dv:dva, 0:512])
                lp.finc_v(e, fep, 1)
            blk.vector(vr)

            def tb2(e, bep=bep):
                e.wait_ge(fep, bep + 2)
                e.matmul(P2[0:dv, 0:512], ones_r[0:1, 0:dv], rcp[:, :],
                         start=True, stop=True)
                lp.finc_t(e, fep, 1)
            blk.tensor(tb2)

            def vo(e, qc=qc, bep=bep):
                e.wait_ge(fep, bep + 3)
                e.tensor_tensor(out_sb[0:dv, qc * 512:(qc + 1) * 512],
                                otmp[0:dv, 0:512], P2[0:dv, 0:512], ALU.mult)
            blk.vector(vo)
        lp.bump(name + "e", 3)



def build_launch_C():
    lp = LP("C")
    SEQL = SEQ
    lq_i = lp.dram_in("lq_i", [2 * 64, SEQL], BF16)
    lk_i = lp.dram_in("lk_i", [2 * 64, SEQL], BF16)
    lv_i = lp.dram_in("lv_i", [128, 2 * 64 * 68], BF16)
    gq_i = lp.dram_in("gq_i", [2 * 64, 2048], BF16)
    gk_i = lp.dram_in("gk_i", [2 * 64, 2048], BF16)
    gv_i = lp.dram_in("gv_i", [128, 2 * 16 * 68], BF16)
    cmask_i = lp.dram_in("cmask_i", [128, 4 * 512], F32)
    xn1 = lp.dram_in("xn1", [128, KD * T], BF16)
    gate_wT = lp.dram_in("gate_wT", [128, KD * 8], F32)
    gate_b = lp.dram_in("gate_b", [1, 8], F32)
    w13 = lp.dram_in("w13", [128, 8 * KD * 1408 * 2], BF16)
    w2e = lp.dram_in("w2e", [128, 8 * 11 * 1024], BF16)
    b13 = lp.dram_in("b13", [128, 8 * 22], F32)
    b2s = lp.dram_in("b2s", [9, 1024], F32)
    sw13 = lp.dram_in("sw13", [128, KD * 1408 * 2], BF16)
    sw2 = lp.dram_in("sw2", [128, 11 * 1024], BF16)
    sb13 = lp.dram_in("sb13", [128, 22], F32)
    ident = lp.dram_in("ident", [128, 128], F32)

    o_lout = lp.dram_out("lout", [2 * 64, SEQL], BF16)
    o_gout = lp.dram_out("gout", [2 * 64, 2048], BF16)
    o_g1 = lp.dram_out("g1", [128, KD * T], BF16)
    o_rw = lp.dram_out("rwd", [9, T], BF16)
    o_lg = lp.dram_out("lgd", [8, T], F32)
    o_pr = lp.dram_out("prd", [128, 16 * 8], F32)
    o_se = lp.dram_out("sed", [128, 16 * 8], F32)
    o_ysh = lp.dram_out("yshd", [128, T], F32)

    lp.sem("st"); lp.sem("ld"); lp.sem("gld")
    lp.setup_consts([1e-5])
    ones_r = lp.sbuf("ones_r", [1, 128], F32)
    cmask = lp.sbuf("cmask", [128, 4 * 512], BF16)
    idnf = lp.sbuf("idnf", [128, 128], F32)
    P0 = lp.psum("P0", [128, 512])
    P1 = lp.psum("P1", [128, 512])
    P2 = lp.psum("P2", [128, 512])
    ldc = dma_in(lp, [(idnf[:, :], ident)], engine="sync")
    ldm = dma_in(lp, [(cmask[:, :], cmask_i)], engine="gpsimd")
    with lp.block() as blk:
        def vinit(e, ldc=ldc, ldm=ldm):
            e.memset(ones_r[:, :], 1.0)
            e.wait_ge(lp.sem("ld"), ldc)
            e.wait_ge(lp.sem("gld"), ldm)
        blk.vector(vinit)

    with lp.scope() as sA:
        qT = lp.sb(sA, "qT", [64, SEQL], BF16)
        kT = lp.sb(sA, "kT", [64, SEQL], BF16)
        vt = lp.sb(sA, "vt", [128, 64 * 68], BF16)
        ob = lp.sb(sA, "ob", [64, SEQL], BF16)
        ptile = lp.sb(sA, "ptile", [128, 2 * 512], BF16)
        otmp = lp.sb(sA, "otmp", [128, 512], F32)
        rcp = lp.sb(sA, "rcp", [1, 512], F32)
        for p in range(2):
            lda = dma_in(lp, [
                (qT[:, :], lq_i[p * 64:(p + 1) * 64, :]),
                (kT[:, :], lk_i[p * 64:(p + 1) * 64, :]),
                (vt[:, :], lv_i[:, p * 64 * 68:(p + 1) * 64 * 68]),
            ], engine="sync")
            wait_stage(lp, "tensor", "ld", lda)
            wait_stage(lp, "scalar", "ld", lda)
            if p > 0:
                wait_stage(lp, "vector", "st", lp.cnt["st"])
            emit_flash(lp, qT=qT[:, :], kT=kT[:, :],
                       v_tm=lambda kt: vt[:, kt * 68:kt * 68 + 65],
                       out_sb=ob[:, :], scale=0.125, Tq=SEQL, dva=65,
                       cmask=cmask[:, :], ones_r=ones_r[:, :],
                       P0=P0, P1=P1, P2=P2, ptile=ptile, otmp=otmp, rcp=rcp,
                       name="fl")
            dma_in(lp, [(o_lout[p * 64:(p + 1) * 64, :], ob[:, :])],
                   engine="sync", sem="st")
        for p in range(2):
            lda = dma_in(lp, [
                (qT[:, 0:2048], gq_i[p * 64:(p + 1) * 64, :]),
                (kT[:, 0:2048], gk_i[p * 64:(p + 1) * 64, :]),
                (vt[:, 0:16 * 68], gv_i[:, p * 16 * 68:(p + 1) * 16 * 68]),
            ], engine="sync")
            wait_stage(lp, "tensor", "ld", lda)
            wait_stage(lp, "scalar", "ld", lda)
            wait_stage(lp, "vector", "st", lp.cnt["st"])
            emit_flash(lp, qT=qT[:, 0:2048], kT=kT[:, 0:2048],
                       v_tm=lambda kt: vt[:, kt * 68:kt * 68 + 65],
                       out_sb=ob[:, 0:2048], scale=0.125, Tq=2048, dva=65,
                       cmask=cmask[:, :], ones_r=ones_r[:, :],
                       P0=P0, P1=P1, P2=P2, ptile=ptile, otmp=otmp, rcp=rcp,
                       name="fl")
            dma_in(lp, [(o_gout[p * 64:(p + 1) * 64, :], ob[:, 0:2048])],
                   engine="sync", sem="st")
    wait_stage(lp, "vector", "st", lp.cnt["st"])

    with lp.scope() as sM:
        xns = lp.sb(sM, "xns", [128, KD * T], BF16)
        ysb = lp.sb(sM, "ysb", [128, KD * T], F32)
        gwT = lp.sb(sM, "gwT", [128, KD * 8], BF16)
        gbr = lp.sb(sM, "gbr", [1, 8], F32)
        gbb = lp.sb(sM, "gbb", [128, 8], F32)
        lgt = lp.sb(sM, "lgt", [8, T], F32)
        rw_tm = lp.sb(sM, "rw_tm", [128, 8], F32)
        prball = lp.sb(sM, "prball", [128, 16 * 8], F32)
        seall = lp.sb(sM, "seall", [128, 16 * 8], F32)
        etall = lp.sb(sM, "etall", [128, 16 * 8], F32)
        rw_all = lp.sb(sM, "rw_all", [128, 16 * 8], F32)
        mxall = lp.sb(sM, "mxall", [128, 32], F32)
        rwT = lp.sb(sM, "rwT9", [9, T], BF16)
        rwb = lp.sb(sM, "rwb", [128, 512], BF16)
        onesb = lp.sb(sM, "onesb", [1, 128], BF16)
        b2sb = lp.sb(sM, "b2sb", [9, 1024], BF16)
        bb = lp.sb(sM, "bb", [128, 22], F32)
        w13s = lp.sb(sM, "w13s", [128, KD * 1408 * 2], BF16)
        w2s = lp.sb(sM, "w2s", [128, 11 * 1024], BF16)
        hbuf = lp.sb(sM, "hbuf", [128, 11 * 512], BF16)
        t1 = lp.sb(sM, "t1", [128, 512], F32)
        t2 = lp.sb(sM, "t2", [128, 512], F32)
        t3 = lp.sb(sM, "t3", [128, 512], F32)
        ldx = dma_in(lp, [(xns[:, :], xn1), (gbr[:, :], gate_b)],
                     engine="sync")
        ldg = dma_in(lp, [(gwT[:, :], gate_wT), (b2sb[:, :], b2s)], engine="gpsimd")
        wait_stage(lp, "tensor", "ld", ldx)
        wait_stage(lp, "tensor", "gld", ldg)

        def rt_ev(m, n, ps, e):
            return e.activation(lgt[:, n * 512:(n + 1) * 512], ps[0:8, :],
                                AF.Identity)

        mm_stage(lp, name="mm",
                 lhsT=lambda k, m: gwT[:, k * 8:(k + 1) * 8],
                 rhs=lambda k, n: xns[:, k * T + n * 512:k * T + (n + 1) * 512],
                 evac=rt_ev, tiles=[(0, n) for n in range(4)],
                 k_tiles=KD, psA=P0, psB=P1, out_part=8)
        bcs = lp.sem("bc")
        with lp.block() as blk:
            def tb(e):
                e.matmul(P2[:, 0:8], ones_r[0:1, 0:128], gbr[:, :],
                         start=True, stop=True)
                lp.finc_t(e, lp.sem("bc"), 1)
            blk.tensor(tb)

            def vb(e):
                e.wait_ge(lp.sem("bc"), lp.cnt["bc"] + 1)
                e.tensor_copy(gbb[:, :], P2[:, 0:8])
            blk.vector(vb)
        lp.bump("bc", 1)

        trs = lp.sem("tr")
        for tt in range(16):
            btr = lp.cnt["tr"]
            with lp.block() as blk:
                def tb(e, tt=tt, btr=btr):
                    e.transpose(P2[:, 0:8], lgt[0:8, tt * 128:(tt + 1) * 128],
                                idnf[0:8, 0:8])
                    lp.finc_t(e, lp.sem("tr"), 1)
                blk.tensor(tb)

                def vb(e, tt=tt, btr=btr):
                    e.wait_ge(lp.sem("tr"), btr + 1)
                    e.tensor_copy(prball[:, tt * 8:(tt + 1) * 8], P2[:, 0:8])
                blk.vector(vb)
            lp.bump("tr", 1)
        W3 = lambda ap: ap.rearrange("p (t e) -> p t e", e=8)
        W4 = lambda ap, w: ap.rearrange("p (t e) -> p t e", e=w)
        with lp.block() as blk:
            def vr(e):
                lg3 = W3(prball[:, :])
                e.tensor_tensor(W4(seall[:, 0:64], 4), lg3[:, :, 0:4], lg3[:, :, 4:8], ALU.max)
                lp.pad_v(e)
                e.tensor_tensor(W4(seall[:, 64:96], 2), W4(seall[:, 0:64], 4)[:, :, 0:2],
                                W4(seall[:, 0:64], 4)[:, :, 2:4], ALU.max)
                lp.pad_v(e)
                e.tensor_tensor(W4(mxall[:, 0:16], 1), W4(seall[:, 64:96], 2)[:, :, 0:1],
                                W4(seall[:, 64:96], 2)[:, :, 1:2], ALU.max)
                lp.pad_v(e)
                lp.pad_v(e)
                e.tensor_tensor(W3(etall[:, :]), lg3,
                                W4(mxall[:, 0:16], 1).to_broadcast([128, 16, 8]), ALU.subtract)
            blk.vector(vr)
        with lp.block() as blk:
            def sc(e):
                e.activation(etall[:, :], etall[:, :], AF.Exp)
            blk.scalar(sc)
        with lp.block() as blk:
            def v2(e):
                et3 = W3(etall[:, :])
                e.tensor_tensor(W4(seall[:, 0:64], 4), et3[:, :, 0:4], et3[:, :, 4:8], ALU.add)
                lp.pad_v(e)
                e.tensor_tensor(W4(seall[:, 64:96], 2), W4(seall[:, 0:64], 4)[:, :, 0:2],
                                W4(seall[:, 0:64], 4)[:, :, 2:4], ALU.add)
                lp.pad_v(e)
                e.tensor_tensor(W4(mxall[:, 0:16], 1), W4(seall[:, 64:96], 2)[:, :, 0:1],
                                W4(seall[:, 64:96], 2)[:, :, 1:2], ALU.add)
                lp.pad_v(e)
                e.reciprocal(mxall[:, 16:32], mxall[:, 0:16])
                lp.pad_v(e)
                lp.pad_v(e)
                e.tensor_tensor(W3(prball[:, :]), et3,
                                W4(mxall[:, 16:32], 1).to_broadcast([128, 16, 8]), ALU.mult)
                lp.pad_v(e)
                e.tensor_tensor(W3(etall[:, :]), W3(prball[:, :]),
                                gbb[:, 0:8].rearrange("p (o e) -> p o e", e=8)
                                .to_broadcast([128, 16, 8]), ALU.add)
                lp.pad_v(e)
                bi3 = W3(etall[:, :])
                e.tensor_tensor(W4(seall[:, 0:64], 4), bi3[:, :, 0:4], bi3[:, :, 4:8], ALU.max)
                lp.pad_v(e)
                e.tensor_tensor(W4(seall[:, 64:96], 2), W4(seall[:, 0:64], 4)[:, :, 0:2],
                                W4(seall[:, 0:64], 4)[:, :, 2:4], ALU.max)
                lp.pad_v(e)
                e.tensor_tensor(W4(mxall[:, 0:16], 1), W4(seall[:, 64:96], 2)[:, :, 0:1],
                                W4(seall[:, 64:96], 2)[:, :, 1:2], ALU.max)
                lp.pad_v(e)
                e.tensor_tensor(W3(seall[:, :]), bi3,
                                W4(mxall[:, 0:16], 1).to_broadcast([128, 16, 8]), ALU.is_ge)
                lp.pad_v(e)
                e.tensor_scalar(rw_all[:, :], seall[:, :], 1e30, None, ALU.mult)
                lp.pad_v(e)
                e.tensor_tensor(etall[:, :], etall[:, :], rw_all[:, :], ALU.subtract)
                lp.pad_v(e)
                bi2 = W3(etall[:, :])
                e.tensor_tensor(W4(rw_all[:, 0:64], 4), bi2[:, :, 0:4], bi2[:, :, 4:8], ALU.max)
                lp.pad_v(e)
                e.tensor_tensor(W4(rw_all[:, 64:96], 2), W4(rw_all[:, 0:64], 4)[:, :, 0:2],
                                W4(rw_all[:, 0:64], 4)[:, :, 2:4], ALU.max)
                lp.pad_v(e)
                e.tensor_tensor(W4(mxall[:, 0:16], 1), W4(rw_all[:, 64:96], 2)[:, :, 0:1],
                                W4(rw_all[:, 64:96], 2)[:, :, 1:2], ALU.max)
                lp.pad_v(e)
                e.tensor_tensor(W3(rw_all[:, :]), bi2,
                                W4(mxall[:, 0:16], 1).to_broadcast([128, 16, 8]), ALU.is_ge)
                lp.pad_v(e)
                e.tensor_tensor(seall[:, :], seall[:, :], rw_all[:, :], ALU.add)
                lp.pad_v(e)
                e.tensor_tensor(rw_all[:, :], seall[:, :], prball[:, :], ALU.mult)
            blk.vector(v2)
        for tt in range(16):
            btr2 = lp.cnt["tr"]
            with lp.block() as blk:
                def tb2(e, tt=tt, btr2=btr2):
                    e.transpose(P2[0:8, 0:128], rw_all[:, tt * 8:(tt + 1) * 8], idnf[:, :])
                    lp.finc_t(e, lp.sem("tr"), 1)
                blk.tensor(tb2)

                def v3(e, tt=tt, btr2=btr2):
                    e.wait_ge(lp.sem("tr"), btr2 + 1)
                    e.tensor_copy(rwT[0:8, tt * 128:(tt + 1) * 128], P2[0:8, 0:128])
                blk.vector(v3)
            lp.bump("tr", 1)
        ones1T = lp.sb(sM, "ones1T", [1, T], BF16)
        with lp.block() as blk:
            def vm(e):
                e.memset(ones1T[:, :], 1.0)
                e.memset(onesb[:, :], 1.0)
            blk.vector(vm)
        dma_in(lp, [(rwT[8:9, 0:T], ones1T[0:1, :])], engine="sync")
        rwE = lp.sb(sM, "rwE", [1, 512], BF16)

        lds = dma_in(lp, [(w13s[:, :], sw13), (w2s[:, :], sw2), (bb[:, 0:22], sb13)],
                     engine="gpsimd")
        wait_stage(lp, "tensor", "gld", lds)
        hsem = lp.sem("hs")
        for n in range(4):
            for mt in range(11):
                bh = lp.cnt["hs"]
                with lp.block() as blk:
                    def tb(e, n=n, mt=mt, bh=bh):
                        for k in range(KD):
                            e.matmul(P0[:, 0:512],
                                     w13s[:, k * 2816 + mt * 128:k * 2816 + (mt + 1) * 128],
                                     xns[:, k * T + n * 512:k * T + (n + 1) * 512],
                                     start=(k == 0), stop=(k == KD - 1))
                        for k in range(KD):
                            e.matmul(P1[:, 0:512],
                                     w13s[:, k * 2816 + 1408 + mt * 128:k * 2816 + 1408 + (mt + 1) * 128],
                                     xns[:, k * T + n * 512:k * T + (n + 1) * 512],
                                     start=(k == 0), stop=(k == KD - 1))
                        lp.finc_t(e, lp.sem("hs"), 1)
                    blk.tensor(tb)

                    def vb(e, n=n, mt=mt, bh=bh):
                        e.wait_ge(lp.sem("hs"), bh + 1)
                        e.tensor_scalar(t1[:, :], P0[:, 0:512], bb[:, mt:mt + 1],
                                        None, ALU.add)
                        e.tensor_scalar(t2[:, :], P1[:, 0:512], bb[:, 11 + mt:12 + mt],
                                        None, ALU.add)
                        lp.pad_v(e)
                        e.tensor_tensor(t3[:, :], t1[:, :], t2[:, :], ALU.mult)
                        lp.finc_v(e, lp.sem("hs"), 1)
                    blk.vector(vb)

                    def sc(e, n=n, mt=mt, bh=bh):
                        e.wait_ge(lp.sem("hs"), bh + 2)
                        e.activation(hbuf[:, mt * 512:(mt + 1) * 512], t3[:, :],
                                     AF.Silu)
                        lp.finc_s(e, lp.sem("hs"), 1)
                    blk.scalar(sc)
                lp.bump("hs", 3)
            for mt in range(KD):
                def y_ev(m, nn, ps, e, mt=mt, n=n):
                    return e.activation(ysb[:, mt * T + n * 512:mt * T + (n + 1) * 512],
                                        ps[:, :], AF.Identity)

                def y_lhs(k, m, mt=mt):
                    if k < 11:
                        return w2s[:, k * 1024 + mt * 128:k * 1024 + (mt + 1) * 128]
                    return b2sb[0:9, mt * 128:(mt + 1) * 128]

                def y_rhs(k, nn, n=n):
                    if k < 11:
                        return hbuf[:, k * 512:(k + 1) * 512]
                    return rwT[0:9, n * 512:(n + 1) * 512]

                mm_stage(lp, name="mm", lhsT=y_lhs, rhs=y_rhs, evac=y_ev,
                         tiles=[(0, 0)], k_tiles=12, psA=P0, psB=P1)
        dma_in(lp, [(o_rw, rwT[0:9, :]), (o_ysh, ysb[:, 0:T]),
                    (o_lg, lgt[0:8, :]), (o_pr, prball[:, :]),
                    (o_se, seall[:, :])],
               engine="sync", sem="st")
        for ee in range(8):
            ldw = dma_in(lp, [
                (w13s[:, :], w13.rearrange("p (e r) -> p e r", e=8)[:, ee, :]),
                (w2s[:, :], w2e.rearrange("p (e r) -> p e r", e=8)[:, ee, :]),
                (bb[:, 0:22], b13.rearrange("p (e r) -> p e r", e=8)[:, ee, :]),
            ], engine="gpsimd")
            wait_stage(lp, "tensor", "gld", ldw)
            for n in range(4):
                rwex = dma_in(lp, [(rwE[0:1, :], rwT[ee:ee + 1, n * 512:(n + 1) * 512])],
                              engine="sync")
                bb3 = lp.cnt["bc"]
                with lp.block() as blk:
                    def tb(e, ee=ee, n=n, bb3=bb3, rwex=rwex):
                        e.wait_ge(lp.sem("ld"), rwex)
                        e.matmul(P2[:, 0:512], onesb[0:1, 0:128],
                                 rwE[0:1, :], start=True, stop=True)
                        lp.finc_t(e, lp.sem("bc"), 1)
                    blk.tensor(tb)

                    def vb(e, bb3=bb3):
                        e.wait_ge(lp.sem("bc"), bb3 + 1)
                        e.tensor_copy(rwb[:, :], P2[:, 0:512])
                    blk.vector(vb)
                lp.bump("bc", 1)
                for mt in range(11):
                    bh = lp.cnt["hs"]
                    with lp.block() as blk:
                        def tb(e, n=n, mt=mt, bh=bh):
                            for k in range(KD):
                                e.matmul(P0[:, 0:512],
                                         w13s[:, k * 2816 + mt * 128:k * 2816 + (mt + 1) * 128],
                                         xns[:, k * T + n * 512:k * T + (n + 1) * 512],
                                         start=(k == 0), stop=(k == KD - 1))
                            for k in range(KD):
                                e.matmul(P1[:, 0:512],
                                         w13s[:, k * 2816 + 1408 + mt * 128:k * 2816 + 1408 + (mt + 1) * 128],
                                         xns[:, k * T + n * 512:k * T + (n + 1) * 512],
                                         start=(k == 0), stop=(k == KD - 1))
                            lp.finc_t(e, lp.sem("hs"), 1)
                        blk.tensor(tb)

                        def vb(e, n=n, mt=mt, bh=bh):
                            e.wait_ge(lp.sem("hs"), bh + 1)
                            e.tensor_scalar(t1[:, :], P0[:, 0:512], bb[:, mt:mt + 1],
                                            None, ALU.add)
                            e.tensor_scalar(t2[:, :], P1[:, 0:512], bb[:, 11 + mt:12 + mt],
                                            None, ALU.add)
                            lp.pad_v(e)
                            e.tensor_tensor(t3[:, :], t1[:, :], t2[:, :], ALU.mult)
                            lp.finc_v(e, lp.sem("hs"), 1)
                        blk.vector(vb)

                        def sc(e, n=n, mt=mt, bh=bh):
                            e.wait_ge(lp.sem("hs"), bh + 2)
                            e.activation(t1[:, :], t3[:, :], AF.Silu)
                            lp.finc_s(e, lp.sem("hs"), 1)
                        blk.scalar(sc)

                        def vb2(e, n=n, mt=mt, bh=bh):
                            e.wait_ge(lp.sem("hs"), bh + 3)
                            e.tensor_tensor(hbuf[:, mt * 512:(mt + 1) * 512],
                                            t1[:, :], rwb[:, :], ALU.mult)
                            lp.finc_v(e, lp.sem("hs"), 1)
                        blk.vector(vb2)
                    lp.bump("hs", 4)
                for mt in range(KD):
                    ya = lp.sem("ya")
                    by = lp.cnt["ya"]
                    with lp.block() as blk:
                        def tb(e, mt=mt, by=by):
                            for k in range(11):
                                e.matmul(P0[:, 0:512],
                                         w2s[:, k * 1024 + mt * 128:k * 1024 + (mt + 1) * 128],
                                         hbuf[:, k * 512:(k + 1) * 512],
                                         start=(k == 0), stop=(k == 10))
                            lp.finc_t(e, lp.sem("ya"), 1)
                        blk.tensor(tb)

                        def vb(e, mt=mt, n=n, by=by):
                            e.wait_ge(lp.sem("ya"), by + 1)
                            e.tensor_tensor(ysb[:, mt * T + n * 512:mt * T + (n + 1) * 512],
                                            ysb[:, mt * T + n * 512:mt * T + (n + 1) * 512],
                                            P0[:, 0:512], ALU.add)
                        blk.vector(vb)
                    lp.bump("ya", 1)
        with lp.block() as blk:
            def sg(e):
                for k in range(KD):
                    for n in range(4):
                        e.activation(ysb[:, k * T + n * 512:k * T + (n + 1) * 512],
                                     ysb[:, k * T + n * 512:k * T + (n + 1) * 512],
                                     AF.Silu)
            blk.scalar(sg)
        dma_in(lp, [(o_g1, ysb[:, :])], engine="gpsimd", sem="st")

    wait_stage(lp, "sync", "st", lp.cnt["st"])
    return lp


def _heads_of_core(c):
    return [(p // 8, p % 8) for p in (2 * c, 2 * c + 1)]


def prep_C(inp, Bo):
    lq_full, gq_full = [], []
    for b in range(2):
        arr = np.concatenate([np.asarray(Bo[4 * b + r]["lqkv"], np.float32)
                              .reshape(128, 12, T).transpose(1, 0, 2).reshape(1536, T)
                              for r in range(4)], -1)
        lq_full.append(arr)
        arrg = np.concatenate([np.asarray(Bo[4 * b + r]["gqkv"], np.float32)
                               .reshape(128, 12, 512).transpose(1, 0, 2).reshape(1536, 512)
                               for r in range(4)], -1)
        gq_full.append(arrg)
    cmask = np.zeros((128, 4 * 512), np.float32)
    for r in range(4):
        k = np.arange(128)[:, None] + r * 128
        q = np.arange(512)[None, :]
        cmask[:, r * 512:(r + 1) * 512] = (k <= q).astype(np.float32)
    gate_wT = _wlay(_f32(inp["L1_ffn_gate_w"]).T)
    gate_b = _f32(inp["L1_ffn_gate_b"]).reshape(1, 8)
    w1 = _f32(inp["L1_ffn_w1"]); w3 = _f32(inp["L1_ffn_w3"]); w2 = _f32(inp["L1_ffn_w2"])
    w13_l, w2_l, b13_l = [], [], []
    for e in range(8):
        a1 = _wlay(w1[e]); a3 = _wlay(w3[e])
        inter = np.concatenate([np.concatenate(
            [a1[:, k * 1408:(k + 1) * 1408], a3[:, k * 1408:(k + 1) * 1408]], 1)
            for k in range(KD)], 1)
        w13_l.append(inter)
        w2_l.append(_wlay(w2[e]))
        b13_l.append(np.concatenate([_pcol(inp["L1_ffn_b1"][e]),
                                     _pcol(inp["L1_ffn_b3"][e])], 1))
    w13 = np.concatenate(w13_l, 1).astype(BFD)
    w2e = np.concatenate(w2_l, 1).astype(BFD)
    b13 = np.concatenate(b13_l, 1)
    b2s = np.concatenate([_f32(inp["L1_ffn_b2"]), _f32(inp["L1_ffn_sb2"]).reshape(1, 1024)], 0)
    a1 = _wlay(inp["L1_ffn_sw1"]); a3 = _wlay(inp["L1_ffn_sw3"])
    sw13 = np.concatenate([np.concatenate(
        [a1[:, k * 1408:(k + 1) * 1408], a3[:, k * 1408:(k + 1) * 1408]], 1)
        for k in range(KD)], 1).astype(BFD)
    sw2 = _wlay(inp["L1_ffn_sw2"]).astype(BFD)
    sb13 = np.concatenate([_pcol(inp["L1_ffn_sb1"]), _pcol(inp["L1_ffn_sb3"])], 1)
    ident = np.eye(128, dtype=np.float32)
    maps = []
    for c in range(NCORES):
        lqs, lks, lvs, gqs, gks, gvs = [], [], [], [], [], []
        for (b, h) in _heads_of_core(c):
            lqs.append(lq_full[b][h * 64:(h + 1) * 64, :])
            lks.append(lq_full[b][512 + h * 64:512 + (h + 1) * 64, :])
            v = lq_full[b][1024 + h * 64:1024 + (h + 1) * 64, :].T
            va = np.concatenate([v, np.ones((SEQ, 1), np.float32),
                                 np.zeros((SEQ, 3), np.float32)], 1)
            lvs.append(np.ascontiguousarray(
                va.reshape(64, 128, 68).transpose(1, 0, 2).reshape(128, 64 * 68)))
            gqs.append(gq_full[b][h * 64:(h + 1) * 64, :])
            gks.append(gq_full[b][512 + h * 64:512 + (h + 1) * 64, :])
            gv = gq_full[b][1024 + h * 64:1024 + (h + 1) * 64, :].T
            gva = np.concatenate([gv, np.ones((2048, 1), np.float32),
                                  np.zeros((2048, 3), np.float32)], 1)
            gvs.append(np.ascontiguousarray(
                gva.reshape(16, 128, 68).transpose(1, 0, 2).reshape(128, 16 * 68)))
        maps.append(dict(
            lq_i=np.concatenate(lqs, 0).astype(BFD),
            lk_i=np.concatenate(lks, 0).astype(BFD),
            lv_i=np.concatenate(lvs, 1).astype(BFD),
            gq_i=np.concatenate(gqs, 0).astype(BFD),
            gk_i=np.concatenate(gks, 0).astype(BFD),
            gv_i=np.concatenate(gvs, 1).astype(BFD),
            cmask_i=cmask, xn1=np.asarray(Bo[c]["xn1"]),
            gate_wT=gate_wT, gate_b=gate_b, w13=w13, w2e=w2e, b13=b13,
            b2s=b2s, sw13=sw13, sw2=sw2, sb13=sb13, ident=ident,
        ))
    return maps



def build_launch_D():
    lp = LP("D")
    x1 = lp.dram_in("x1", [128, KD * T], F32)
    g1b = lp.dram_in("g1b", [128, KD * T], BF16)
    loutT = lp.dram_in("loutT", [128, 4 * T], BF16)
    goutT = lp.dram_in("goutT", [128, 4 * 512], BF16)
    wlp = lp.dram_in("wlp", [128, 4 * 512], BF16)
    blp = lp.dram_in("blp", [128, 4], F32)
    wgp = lp.dram_in("wgp", [128, 4 * 512], BF16)
    bgp = lp.dram_in("bgp", [128, 4], F32)
    wgt = lp.dram_in("wgt", [128, 8 * 2], F32)
    bg2 = lp.dram_in("bg2", [1, 2], F32)
    wqT = lp.dram_in("wqT", [128, 8 * 1024], BF16)
    bq = lp.dram_in("bq", [128, 8], F32)
    wkfm = lp.dram_in("wkfm", [128, 8 * 1024], BF16)
    wvfm = lp.dram_in("wvfm", [128, 8 * 1024], BF16)
    bkrow = lp.dram_in("bkrow", [1, 1024], BF16)
    bvrow = lp.dram_in("bvrow", [1, 1024], BF16)
    wln1 = lp.dram_in("wln1", [128, 8], F32)
    wqi = lp.dram_in("wqi", [128, 8 * 1024], BF16)
    wkva_lat = lp.dram_in("wkva_lat", [128, 8 * 128], BF16)
    wkva_kr = lp.dram_in("wkva_kr", [128, 8 * 64], BF16)
    wkvb = lp.dram_in("wkvb", [128, 512], BF16)
    cosI = lp.dram_in("cosI", [128, T], F32)
    sinI = lp.dram_in("sinI", [128, T], F32)

    o_x2 = lp.dram_out("x2", [128, KD * T], F32)
    o_xn2f = lp.dram_out("xn2f", [128, KD * T], F32)
    o_xn2b = lp.dram_out("xn2b", [128, KD * T], BF16)
    o_q2 = lp.dram_out("q2", [128, KD * T], BF16)
    o_kvp = lp.dram_out("kvp", [128, 2 * 512], F32)
    o_rq = lp.dram_out("rq", [128, KD * T], BF16)
    o_kvn = lp.dram_out("kvn", [128, 4 * T], BF16)
    o_kr = lp.dram_out("kr", [64, T], BF16)

    lp.sem("st"); lp.sem("ld"); lp.sem("gld")
    lp.setup_consts([1e-5])
    P0 = lp.psum("P0", [128, 512])
    P1 = lp.psum("P1", [128, 512])
    P2 = lp.psum("P2", [128, 512])
    P3 = lp.psum("P3", [128, 512])
    P4 = lp.psum("P4", [128, 512])
    KV0 = lp.psum("KV0", [128, 512])
    KV1 = lp.psum("KV1", [128, 512])

    xb = lp.sbuf("xb", [128, KD * 512], BF16)
    sq = lp.sbuf("sq", [128, KD * 512], BF16)
    sS1 = lp.sbuf("sS1", [1, 512], F32)
    sS2 = lp.sbuf("sS2", [1, 512], F32)
    sAR = lp.sbuf("sAR", [1, 512], F32)
    sBR = lp.sbuf("sBR", [1, 512], F32)
    sM2 = lp.sbuf("sM2", [1, 512], F32)
    ones_c = lp.sbuf("ones_c", [128, 2], BF16)
    ones_r = lp.sbuf("ones_r", [1, 128], F32)
    ones_rb = lp.sbuf("ones_rb", [1, 128], BF16)
    ones_colb = lp.sbuf("ones_colb", [128, 1], BF16)
    scr = {"xb": xb, "sq": sq, "ones_col": ones_c[:, 0:1],
           "ones_row": ones_r[:, :], "P0": P0, "P1": P1, "PS": P2[0:1, :],
           "S1": sS1, "S2": sS2, "AR": sAR, "BR": sBR, "M2": sM2}
    wlp_s = lp.sbuf("wlp_s", [128, 4 * 512], BF16)
    wgp_s = lp.sbuf("wgp_s", [128, 4 * 512], BF16)
    wgt_s = lp.sbuf("wgt_s", [128, 8 * 2], F32)
    bg2_s = lp.sbuf("bg2_s", [1, 2], F32)
    wq_s = lp.sbuf("wq_s", [128, 8 * 1024], BF16)
    wkf_s = lp.sbuf("wkf_s", [128, 8 * 1024], BF16)
    wvf_s = lp.sbuf("wvf_s", [128, 8 * 1024], BF16)
    bk_s = lp.sbuf("bk_s", [1, 1024], BF16)
    bv_s = lp.sbuf("bv_s", [1, 1024], BF16)
    wqi_s = lp.sbuf("wqi_s", [128, 8 * 1024], BF16)
    wkl_s = lp.sbuf("wkl_s", [128, 8 * 128], BF16)
    wkr_s = lp.sbuf("wkr_s", [128, 8 * 64], BF16)
    wkvb_s = lp.sbuf("wkvb_s", [128, 512], BF16)
    biasb = lp.sbuf("biasb", [128, 32], F32)

    ld1 = dma_in(lp, [
        (wlp_s[:, :], wlp), (wgp_s[:, :], wgp), (wgt_s[:, :], wgt),
        (bg2_s[:, :], bg2), (biasb[:, 0:4], blp), (biasb[:, 4:8], bgp),
        (biasb[:, 8:16], bq), (biasb[:, 16:24], wln1),
        (bk_s[:, :], bkrow), (bv_s[:, :], bvrow),
    ], engine="sync")
    lw1 = dma_in(lp, [
        (wq_s[:, :], wqT), (wkf_s[:, :], wkfm), (wvf_s[:, :], wvfm),
        (wqi_s[:, :], wqi), (wkl_s[:, :], wkva_lat), (wkr_s[:, :], wkva_kr),
        (wkvb_s[:, :], wkvb),
    ], engine="gpsimd")
    with lp.block() as blk:
        def vinit(e, ld1=ld1, lw1=lw1):
            e.memset(ones_c[:, :], 1.0)
            e.memset(ones_r[:, :], 1.0)
            e.memset(ones_rb[:, :], 1.0)
            e.memset(ones_colb[:, :], 1.0)
            e.wait_ge(lp.sem("ld"), ld1)
            e.wait_ge(lp.sem("gld"), lw1)
        blk.vector(vinit)

    for n in range(4):
        tsl = slice(n * 512, (n + 1) * 512)
        if n > 0:
            with lp.block() as blk:
                def w1(e):
                    e.wait_ge(lp.sem("st"), lp.cnt["st"])
                blk.vector(w1)
                def w2(e):
                    e.wait_ge(lp.sem("st"), lp.cnt["st"])
                blk.scalar(w2)
        with lp.scope() as sB:
            x2c = lp.sb(sB, "x2c", [128, KD * 512], F32)
            xn2bc = lp.sb(sB, "xn2bc", [128, KD * 512], BF16)
            xib = lp.sb(sB, "xib", [128, KD * 512], BF16)
            tt1 = lp.sb(sB, "tt1", [128, 512], F32)
            tt2 = lp.sb(sB, "tt2", [128, 512], F32)
            with lp.scope() as s1:
                g1f = lp.sb(s1, "g1f", [128, KD * 512], F32)
                loc = lp.sb(s1, "loc", [128, 4 * 512], BF16)
                goc = lp.sb(s1, "goc", [128, 4 * 128], BF16)
                lopc = lp.sb(s1, "lopc", [128, 4 * 512], F32)
                gopc = lp.sb(s1, "gopc", [128, 4 * 128], F32)
                gopr = lp.sb(s1, "gopr", [128, 4 * 512], F32)
                gl0 = lp.sb(s1, "gl0", [1, 512], F32)
                gl1 = lp.sb(s1, "gl1", [1, 512], F32)
                gld_ = lp.sb(s1, "gld_", [1, 512], F32)
                gsl = lp.sb(s1, "gsl", [1, 512], F32)
                gsg = lp.sb(s1, "gsg", [1, 512], F32)
                gbl = lp.sb(s1, "gbl", [128, 512], F32)
                gbg = lp.sb(s1, "gbg", [128, 512], F32)
                ld2 = dma_in(lp, [
                    (x2c[:, :], x1.rearrange("p (k t) -> p k t", t=T)[:, :, tsl]),
                    (loc[:, :], loutT.rearrange("p (k t) -> p k t", t=T)[:, :, tsl]),
                    (goc[:, :], goutT.rearrange("p (k t) -> p k t", t=512)[:, :, n * 128:(n + 1) * 128]),
                ], engine="sync")
                ld2g = dma_in(lp, [
                    (g1f[:, :], g1b.rearrange("p (k t) -> p k t", t=T)[:, :, tsl]),
                ], engine="gpsimd")
                wait_stage(lp, "tensor", "ld", ld2)
                wait_stage(lp, "vector", "ld", ld2)
                wait_stage(lp, "vector", "gld", ld2g)

                def lp_ev(m, nn, ps, e):
                    return e.activation(lopc[:, m * 512:(m + 1) * 512], ps[:, :],
                                        AF.Identity, bias=biasb[:, m:m + 1])
                mm_stage(lp, name="mm",
                         lhsT=lambda k, m: wlp_s[:, k * 512 + m * 128:k * 512 + (m + 1) * 128],
                         rhs=lambda k, nn: loc[:, k * 512:(k + 1) * 512],
                         evac=lp_ev, tiles=[(m, 0) for m in range(4)],
                         k_tiles=4, psA=P0, psB=P1)

                def gp_ev(m, nn, ps, e):
                    return e.activation(gopc[:, m * 128:(m + 1) * 128], ps[:, 0:128],
                                        AF.Identity, bias=biasb[:, 4 + m:5 + m])
                mm_stage(lp, name="mm",
                         lhsT=lambda k, m: wgp_s[:, k * 512 + m * 128:k * 512 + (m + 1) * 128],
                         rhs=lambda k, nn: goc[:, k * 128:(k + 1) * 128],
                         evac=gp_ev, tiles=[(m, 0) for m in range(4)],
                         k_tiles=4, psA=P0, psB=P1, out_free=128)
                with lp.block() as blk:
                    def vrep(e):
                        for k in range(4):
                            e.tensor_copy(
                                gopr[:, k * 512:(k + 1) * 512].rearrange("p (g r) -> p g r", r=4),
                                gopc[:, k * 128:(k + 1) * 128].rearrange("p (g o) -> p g o", o=1)
                                .to_broadcast([128, 128, 4]))
                    blk.vector(vrep)

                def comb_rhs(k, nn):
                    return lopc[:, k * 512:(k + 1) * 512] if k < 4 else \
                        gopr[:, (k - 4) * 512:(k - 3) * 512]

                def g0_ev(m, nn, ps, e):
                    return e.activation(gl0[:, :], ps[0:1, :], AF.Identity,
                                        bias=bg2_s[0:1, 0:1])
                mm_stage(lp, name="mm",
                         lhsT=lambda k, m: wgt_s[:, k * 2:k * 2 + 1],
                         rhs=comb_rhs, evac=g0_ev, tiles=[(0, 0)],
                         k_tiles=8, psA=P0, psB=P1, out_part=1)

                def g1_ev(m, nn, ps, e):
                    return e.activation(gl1[:, :], ps[0:1, :], AF.Identity,
                                        bias=bg2_s[0:1, 1:2])
                mm_stage(lp, name="mm",
                         lhsT=lambda k, m: wgt_s[:, k * 2 + 1:k * 2 + 2],
                         rhs=comb_rhs, evac=g1_ev, tiles=[(0, 0)],
                         k_tiles=8, psA=P0, psB=P1, out_part=1)
                with lp.block() as blk:
                    def vd(e):
                        e.tensor_tensor(gld_[:, :], gl0[:, :], gl1[:, :], ALU.subtract)
                    blk.vector(vd)
                with lp.block() as blk:
                    def sd(e):
                        e.activation(gsl[:, :], gld_[:, :], AF.Sigmoid)
                        lp.pad_s(e)
                        e.activation(gsg[:, :], gld_[:, :], AF.Sigmoid, scale=-1.0)
                    blk.scalar(sd)
                mm_stage(lp, name="mm",
                         lhsT=lambda k, m: ones_r[0:1, :],
                         rhs=lambda k, nn: gsl[:, :],
                         evac=lambda m, nn, ps, e: e.activation(gbl[:, :], ps[:, :], AF.Identity),
                         tiles=[(0, 0)], k_tiles=1, psA=P0, psB=P1)
                mm_stage(lp, name="mm",
                         lhsT=lambda k, m: ones_r[0:1, :],
                         rhs=lambda k, nn: gsg[:, :],
                         evac=lambda m, nn, ps, e: e.activation(gbg[:, :], ps[:, :], AF.Identity),
                         tiles=[(0, 0)], k_tiles=1, psA=P0, psB=P1)
                with lp.block() as blk:
                    def vx2(e):
                        for k in range(KD):
                            if k < 4:
                                e.tensor_tensor(tt1[:, :], lopc[:, k * 512:(k + 1) * 512],
                                                gbl[:, :], ALU.mult)
                            else:
                                e.tensor_tensor(tt1[:, :], gopr[:, (k - 4) * 512:(k - 3) * 512],
                                                gbg[:, :], ALU.mult)
                            e.tensor_tensor(tt2[:, :], tt1[:, :],
                                            g1f[:, k * 512:(k + 1) * 512], ALU.mult)
                            e.tensor_tensor(x2c[:, k * 512:(k + 1) * 512],
                                            x2c[:, k * 512:(k + 1) * 512],
                                            tt2[:, :], ALU.add)
                    blk.vector(vx2)
            dma_in(lp, [(o_x2.rearrange("p (k t) -> p k t", t=T)[:, :, tsl], x2c[:, :])],
                   engine="sync", sem="st")

            with lp.scope() as s15:
                xn2fc = lp.sb(s15, "xn2fc", [128, KD * 512], F32)

                def xn2_apply(k, nn, sl2, e):
                    e.tensor_tensor(tt1[:, :], x2c[:, k * 512:(k + 1) * 512],
                                    P0[:, 0:512], ALU.mult)
                    e.tensor_tensor(xn2fc[:, k * 512:(k + 1) * 512], tt1[:, :],
                                    P1[:, 0:512], ALU.add)
                norm_chunked(lp, x_of=lambda k: x2c[:, k * 512:(k + 1) * 512],
                             out_emit=xn2_apply, scr=scr, kind="ln", name="nrm",
                             t_len=512, chunk=512)
                with lp.block() as blk:
                    def vcb(e):
                        e.tensor_copy(xn2bc[:, :], xn2fc[:, :])
                    blk.vector(vcb)
                dma_in(lp, [(o_xn2f.rearrange("p (k t) -> p k t", t=T)[:, :, tsl], xn2fc[:, :]),
                            (o_xn2b.rearrange("p (k t) -> p k t", t=T)[:, :, tsl], xn2bc[:, :])],
                       engine="sync", sem="st")

                def xi_apply(k, nn, sl2, e):
                    e.tensor_tensor(tt1[:, :], xn2fc[:, k * 512:(k + 1) * 512],
                                    P0[:, 0:512], ALU.mult)
                    e.tensor_scalar(xib[:, k * 512:(k + 1) * 512], tt1[:, :],
                                    biasb[:, 16 + k:17 + k], None, ALU.mult)
                norm_chunked(lp, x_of=lambda k: xn2fc[:, k * 512:(k + 1) * 512],
                             out_emit=xi_apply, scr=scr, kind="rms", name="nrm",
                             t_len=512, chunk=512)
                wait_stage(lp, "vector", "st", lp.cnt["st"])
                wait_stage(lp, "scalar", "st", lp.cnt["st"])

            with lp.scope() as s2:
                cosc = lp.sb(s2, "cosc", [128, 512], F32)
                sinc = lp.sb(s2, "sinc", [128, 512], F32)
                ldcs = dma_in(lp, [(cosc[:, :], cosI[:, tsl]),
                                   (sinc[:, :], sinI[:, tsl])], engine="sync")
                wait_stage(lp, "vector", "ld", ldcs)
                qsb = lp.sb(s2, "qsb", [128, KD * 512], BF16)
                sQ = lp.scope()
                sQ.__enter__()
                e1t = lp.sb(sQ, "e1t", [128, KD * 512], BF16)
                e2t = lp.sb(sQ, "e2t", [128, KD * 512], BF16)
                e3t = lp.sb(sQ, "e3t", [128, 512], F32)

                def q_ev(m, nn, ps, e):
                    e.activation(e3t[:, :], ps[:, :],
                                 AF.Identity, bias=biasb[:, 8 + m:9 + m])
                    lp.pad_s(e)
                    e.activation(e1t[:, m * 512:(m + 1) * 512],
                                 e3t[:, :], AF.Relu, scale=-1.0)
                    lp.pad_s(e)
                    e.activation(e1t[:, m * 512:(m + 1) * 512],
                                 e1t[:, m * 512:(m + 1) * 512], AF.Exp, scale=-1.0)
                    lp.pad_s(e)
                    return e.activation(e2t[:, m * 512:(m + 1) * 512],
                                        e3t[:, :], AF.Relu)
                mm_stage(lp, name="mm",
                         lhsT=lambda k, m: wq_s[:, k * 1024 + m * 128:k * 1024 + (m + 1) * 128],
                         rhs=lambda k, nn: xn2bc[:, k * 512:(k + 1) * 512],
                         evac=q_ev, tiles=[(m, 0) for m in range(8)],
                         k_tiles=8, psA=P0, psB=P1)
                with lp.block() as blk:
                    def vq(e):
                        for m in range(8):
                            e.tensor_tensor(qsb[:, m * 512:(m + 1) * 512],
                                            e1t[:, m * 512:(m + 1) * 512],
                                            e2t[:, m * 512:(m + 1) * 512], ALU.add)
                    blk.vector(vq)
                sQ.__exit__(None, None, None)
                dma_in(lp, [(o_q2.rearrange("p (k t) -> p k t", t=T)[:, :, tsl], qsb[:, :])],
                       engine="sync", sem="st")

                ktm = lp.sb(s2, "ktm", [128, 1024], BF16)
                vtm = lp.sb(s2, "vtm", [128, 1024], BF16)
                kf1 = lp.sb(s2, "kf1", [128, 1024], F32)
                kf2 = lp.sb(s2, "kf2", [128, 1024], F32)
                kvs_ = lp.sem("kvs")
                for sc in range(4):
                    gsc = n * 4 + sc
                    b0 = lp.cnt["kvs"]
                    with lp.block() as blk:
                        def tb(e, sc=sc, b0=b0):
                            for half in range(2):
                                pdst = [P3, P4][half]
                                for k in range(KD):
                                    e.matmul(pdst[:, 0:512],
                                             xn2bc[:, k * 512 + sc * 128:k * 512 + (sc + 1) * 128],
                                             wkf_s[:, k * 1024 + half * 512:k * 1024 + (half + 1) * 512],
                                             start=(k == 0), stop=False)
                                e.matmul(pdst[:, 0:512], ones_rb[0:1, 0:128],
                                         bk_s[0:1, half * 512:(half + 1) * 512],
                                         start=False, stop=True)
                            lp.finc_t(e, kvs_, 1)
                        blk.tensor(tb)

                        def sb_(e, sc=sc, b0=b0):
                            e.wait_ge(kvs_, b0 + 1)
                            for half in range(2):
                                psrc = [P3, P4][half]
                                hs = slice(half * 512, (half + 1) * 512)
                                e.activation(kf1[:, hs], psrc[:, 0:512], AF.Relu, scale=-1.0)
                                lp.pad_s(e)
                                e.activation(kf1[:, hs], kf1[:, hs], AF.Exp, scale=-1.0)
                                lp.pad_s(e)
                                e.activation(kf2[:, hs], psrc[:, 0:512], AF.Relu)
                            lp.finc_s(e, kvs_, 1)
                        blk.scalar(sb_)
                    lp.bump("kvs", 2)
                    b1 = lp.cnt["kvs"]
                    with lp.block() as blk:
                        def tb2(e, sc=sc, b1=b1):
                            for half in range(2):
                                pdst = [P3, P4][half]
                                for k in range(KD):
                                    e.matmul(pdst[:, 0:512],
                                             xn2bc[:, k * 512 + sc * 128:k * 512 + (sc + 1) * 128],
                                             wvf_s[:, k * 1024 + half * 512:k * 1024 + (half + 1) * 512],
                                             start=(k == 0), stop=False)
                                e.matmul(pdst[:, 0:512], ones_rb[0:1, 0:128],
                                         bv_s[0:1, half * 512:(half + 1) * 512],
                                         start=False, stop=True)
                            lp.finc_t(e, kvs_, 1)
                        blk.tensor(tb2)

                        def vk(e, sc=sc, b1=b1):
                            e.tensor_tensor(ktm[:, :], kf1[:, :], kf2[:, :], ALU.add)
                        blk.vector(vk)

                        def sv(e, sc=sc, b1=b1):
                            e.wait_ge(kvs_, b1 + 1)
                            e.activation(vtm[:, 0:512], P3[:, 0:512], AF.Identity)
                            e.activation(vtm[:, 512:1024], P4[:, 0:512], AF.Identity)
                        blk.scalar(sv)
                    lp.bump("kvs", 1)
                    with lp.block() as blk:
                        def tkv(e, gsc=gsc):
                            for h in range(16):
                                pdst = KV0 if h < 8 else KV1
                                hh = h % 8
                                rr = slice(64 * (hh % 2), 64 * (hh % 2) + 64)
                                c0 = (hh // 2) * 128
                                e.matmul(pdst[rr, c0:c0 + 64],
                                         ktm[:, h * 64:(h + 1) * 64],
                                         vtm[:, h * 64:(h + 1) * 64],
                                         start=(gsc == 0), stop=(gsc == 15))
                                e.matmul(pdst[rr, c0 + 64:c0 + 65],
                                         ktm[:, h * 64:(h + 1) * 64],
                                         ones_colb[:, 0:1],
                                         start=(gsc == 0), stop=(gsc == 15))
                            if gsc == 15:
                                lp.finc_t(e, kvs_, 1)
                        blk.tensor(tkv)
                    if gsc == 15:
                        b2 = lp.cnt["kvs"]
                        lp.bump("kvs", 1)
                        with lp.block() as blk:
                            def skv(e, b2=b2):
                                e.wait_ge(kvs_, b2 + 1)
                                e.activation(kf1[:, 0:512], KV0[:, 0:512], AF.Identity)
                                e.activation(kf1[:, 512:1024], KV1[:, 0:512], AF.Identity)
                            blk.scalar(skv)
                        dma_in(lp, [(o_kvp[:, 0:512], kf1[:, 0:512]),
                                    (o_kvp[:, 512:1024], kf1[:, 512:1024])],
                               engine="sync", sem="st")

                rqc = lp.sb(s2, "rqc", [128, 4 * 512], F32)
                rqs = lp.sb(s2, "rqs", [128, KD * 512], BF16)

                def rq_ev(m, nn, ps, e):
                    if m < 4:
                        return e.activation(rqs[:, m * 512:(m + 1) * 512], ps[:, :],
                                            AF.Identity)
                    return e.activation(rqc[:, (m - 4) * 512:(m - 3) * 512], ps[:, :],
                                        AF.Identity)
                mm_stage(lp, name="mm",
                         lhsT=lambda k, m: wqi_s[:, k * 1024 + m * 128:k * 1024 + (m + 1) * 128],
                         rhs=lambda k, nn: xib[:, k * 512:(k + 1) * 512],
                         evac=rq_ev, tiles=[(m, 0) for m in range(8)],
                         k_tiles=8, psA=P0, psB=P1)
                with lp.block() as blk:
                    def vrope(e):
                        for j in range(2):
                            x0 = rqc[:, j * 512:(j + 1) * 512]
                            x1_ = rqc[:, (2 + j) * 512:(3 + j) * 512]
                            e.tensor_tensor(tt1[:, :], x0, cosc[:, :], ALU.mult)
                            e.tensor_tensor(tt2[:, :], x1_, sinc[:, :], ALU.mult)
                            e.tensor_tensor(rqs[:, (4 + j) * 512:(5 + j) * 512],
                                            tt1[:, :], tt2[:, :], ALU.subtract)
                            e.tensor_tensor(tt1[:, :], x0, sinc[:, :], ALU.mult)
                            e.tensor_tensor(tt2[:, :], x1_, cosc[:, :], ALU.mult)
                            e.tensor_tensor(rqs[:, (6 + j) * 512:(7 + j) * 512],
                                            tt1[:, :], tt2[:, :], ALU.add)
                    blk.vector(vrope)
                dma_in(lp, [(o_rq.rearrange("p (k t) -> p k t", t=T)[:, :, tsl], rqs[:, :])],
                       engine="sync", sem="st")

                latb = lp.sb(s2, "latb", [128, 512], BF16)
                krc = lp.sb(s2, "krc", [64, 512], F32)
                kr2 = lp.sb(s2, "kr2", [16, 512], F32)
                krs = lp.sb(s2, "krs", [64, 512], BF16)
                kvnc = lp.sb(s2, "kvnc", [128, 4 * 512], BF16)
                mm_stage(lp, name="mm",
                         lhsT=lambda k, m: wkl_s[:, k * 128:(k + 1) * 128],
                         rhs=lambda k, nn: xib[:, k * 512:(k + 1) * 512],
                         evac=lambda m, nn, ps, e: e.activation(latb[:, :], ps[:, :], AF.Identity),
                         tiles=[(0, 0)], k_tiles=8, psA=P0, psB=P1)
                mm_stage(lp, name="mm",
                         lhsT=lambda k, m: wkr_s[:, k * 64:(k + 1) * 64],
                         rhs=lambda k, nn: xib[:, k * 512:(k + 1) * 512],
                         evac=lambda m, nn, ps, e: e.activation(krc[:, :], ps[0:64, :], AF.Identity),
                         tiles=[(0, 0)], k_tiles=8, psA=P0, psB=P1, out_part=64)
                with lp.block() as blk:
                    def vkr(e):
                        e.tensor_copy(kr2[:, :], krc[32:48, :])
                        lp.pad_v(e)
                        e.tensor_tensor(tt1[0:16, :], krc[0:16, :], cosc[0:16, :], ALU.mult)
                        e.tensor_tensor(tt2[0:16, :], kr2[:, :], sinc[0:16, :], ALU.mult)
                        lp.pad_v(e)
                        e.tensor_tensor(krs[0:16, :], tt1[0:16, :], tt2[0:16, :], ALU.subtract)
                        e.tensor_tensor(tt1[0:16, :], krc[0:16, :], sinc[0:16, :], ALU.mult)
                        e.tensor_tensor(tt2[0:16, :], kr2[:, :], cosc[0:16, :], ALU.mult)
                        lp.pad_v(e)
                        e.tensor_tensor(krs[32:48, :], tt1[0:16, :], tt2[0:16, :], ALU.add)
                    blk.vector(vkr)
                dma_in(lp, [(o_kr[:, tsl], krs[:, :])], engine="sync", sem="st")
                mm_stage(lp, name="mm",
                         lhsT=lambda k, m: wkvb_s[:, m * 128:(m + 1) * 128],
                         rhs=lambda k, nn: latb[:, :],
                         evac=lambda m, nn, ps, e: e.activation(
                             kvnc[:, m * 512:(m + 1) * 512], ps[:, :], AF.Identity),
                         tiles=[(m, 0) for m in range(4)], k_tiles=1, psA=P0, psB=P1)
                dma_in(lp, [(o_kvn.rearrange("p (k t) -> p k t", t=T)[:, :, tsl], kvnc[:, :])],
                       engine="sync", sem="st")
                wait_stage(lp, "vector", "st", lp.cnt["st"])
                wait_stage(lp, "scalar", "st", lp.cnt["st"])
        wait_stage(lp, "sync", "st", lp.cnt["st"])
    return lp



def build_launch_E():
    lp = LP("E")
    q2 = lp.dram_in("q2", [128, KD * T], BF16)
    kvp4 = lp.dram_in("kvp4", [64, 4 * 16 * 65], F32)
    wc_l = lp.dram_in("wc_l", [128, 8 * 1024], BF16)
    bc = lp.dram_in("bc", [128, 8], F32)
    qi_i = lp.dram_in("qi_i", [64, 4 * SEQ], BF16)
    ki_i = lp.dram_in("ki_i", [64, 4 * SEQ], BF16)
    vi_i = lp.dram_in("vi_i", [128, 4 * 64 * 36], BF16)
    cmask_i = lp.dram_in("cmask_i", [128, 4 * 512], F32)
    o_s2 = lp.dram_out("s2", [128, KD * T], F32)
    o_in = lp.dram_out("inq", [128, SEQ], BF16)
    lp.sem("st"); lp.sem("ld"); lp.sem("gld")
    lp.setup_consts([1e-5])
    P0 = lp.psum("P0", [128, 512])
    P1 = lp.psum("P1", [128, 512])
    P2 = lp.psum("P2", [128, 512])
    ones_r = lp.sbuf("ones_r", [1, 128], F32)
    cmask = lp.sbuf("cmask", [128, 4 * 512], BF16)
    ldm = dma_in(lp, [(cmask[:, :], cmask_i)], engine="gpsimd")
    with lp.block() as blk:
        def vinit(e, ldm=ldm):
            e.memset(ones_r[:, :], 1.0)
            e.wait_ge(lp.sem("gld"), ldm)
        blk.vector(vinit)

    with lp.scope() as sA:
        q2s = lp.sb(sA, "q2s", [128, KD * T], BF16)
        kvp = lp.sb(sA, "kvp", [64, 4 * 16 * 65], F32)
        kvsf = lp.sb(sA, "kvsf", [64, 16 * 65], F32)
        kvsb = lp.sb(sA, "kvsb", [128, 8 * 65], BF16)
        wc_s = lp.sb(sA, "wc_s", [128, 8 * 1024], BF16)
        bcs = lp.sb(sA, "bcs", [128, 8], F32)
        ysb = lp.sb(sA, "ysb", [128, KD * T], BF16)
        otmp = lp.sb(sA, "otmp", [128, 512], F32)
        rcp = lp.sb(sA, "rcp", [1, 512], F32)
        s2c = lp.sb(sA, "s2c", [128, KD * 512], F32)
        ld1 = dma_in(lp, [(q2s[:, :], q2), (kvp[:, :], kvp4), (bcs[:, :], bc)],
                     engine="sync")
        lw1 = dma_in(lp, [(wc_s[:, :], wc_l)], engine="gpsimd")
        with lp.block() as blk:
            def vsum(e, ld1=ld1):
                e.wait_ge(lp.sem("ld"), ld1)
                e.tensor_tensor(kvsf[:, :], kvp[:, 0:1040], kvp[:, 1040:2080], ALU.add)
                lp.pad_v(e)
                e.tensor_tensor(kvsf[:, :], kvsf[:, :], kvp[:, 2080:3120], ALU.add)
                lp.pad_v(e)
                e.tensor_tensor(kvsf[:, :], kvsf[:, :], kvp[:, 3120:4160], ALU.add)
                lp.pad_v(e)
                lp.pad_v(e)
                for h in range(16):
                    e.tensor_copy(kvsb[64 * (h % 2):64 * (h % 2) + 64,
                                       (h // 2) * 65:(h // 2 + 1) * 65],
                                  kvsf[:, h * 65:(h + 1) * 65])
                    lp.pad_v(e)
            blk.vector(vsum)
        le = lp.sem("le")
        for h in range(16):
            kt_ = h // 2
            rr = slice(64 * (h % 2), 64 * (h % 2) + 64)
            for nn in range(4):
                b0 = lp.cnt["le"]
                with lp.block() as blk:
                    def tb(e, h=h, nn=nn, kt_=kt_, rr=rr, b0=b0):
                        e.matmul(P0[0:65, 0:512],
                                 kvsb[64 * (h % 2):64 * (h % 2) + 64,
                                      (h // 2) * 65:(h // 2 + 1) * 65],
                                 q2s[rr, kt_ * T + nn * 512:kt_ * T + (nn + 1) * 512],
                                 start=True, stop=True)
                        lp.finc_t(e, le, 1)
                    blk.tensor(tb)

                    def sb_(e, b0=b0):
                        e.wait_ge(le, b0 + 1)
                        e.activation(otmp[0:65, :], P0[0:65, 0:512], AF.Identity)
                        lp.finc_s(e, le, 1)
                    blk.scalar(sb_)
                lp.bump("le", 2)
                b1 = lp.cnt["le"]
                with lp.block() as blk:
                    def vr(e, b1=b1):
                        e.tensor_scalar(rcp[:, :], otmp[64:65, :], 1e-6, None, ALU.add)
                        lp.pad_v(e)
                        e.reciprocal(rcp[:, :], rcp[:, :])
                        lp.finc_v(e, le, 1)
                        lp.pad_v(e)
                        lp.pad_v(e)
                    blk.vector(vr)

                    def tb2(e, b1=b1):
                        e.wait_ge(le, b1 + 1)
                        e.matmul(P2[0:64, 0:512], ones_r[0:1, 0:64], rcp[:, :],
                                 start=True, stop=True)
                        lp.finc_t(e, le, 1)
                    blk.tensor(tb2)

                    def vo(e, h=h, nn=nn, kt_=kt_, rr=rr, b1=b1):
                        e.wait_ge(le, b1 + 2)
                        e.tensor_tensor(ysb[rr, kt_ * T + nn * 512:kt_ * T + (nn + 1) * 512],
                                        otmp[0:64, :], P2[0:64, 0:512], ALU.mult)
                    blk.vector(vo)
                lp.bump("le", 2)
        for nn in range(4):
            tsl = slice(nn * 512, (nn + 1) * 512)
            if nn > 0:
                wait_stage(lp, "scalar", "st", lp.cnt["st"])

            def wc_ev(m, x, ps, e):
                return e.activation(s2c[:, m * 512:(m + 1) * 512], ps[:, :],
                                    AF.Identity, bias=bcs[:, m:m + 1])
            mm_stage(lp, name="mm",
                     lhsT=lambda k, m: wc_s[:, k * 1024 + m * 128:k * 1024 + (m + 1) * 128],
                     rhs=lambda k, x, nn=nn: ysb[:, k * T + nn * 512:k * T + (nn + 1) * 512],
                     evac=wc_ev, tiles=[(m, 0) for m in range(8)],
                     k_tiles=8, psA=P0, psB=P1)
            dma_in(lp, [(o_s2.rearrange("p (k t) -> p k t", t=T)[:, :, tsl], s2c[:, :])],
                   engine="sync", sem="st")
    wait_stage(lp, "vector", "st", lp.cnt["st"])

    with lp.scope() as sF:
        qT = lp.sb(sF, "qT", [64, SEQ], BF16)
        kT = lp.sb(sF, "kT", [64, SEQ], BF16)
        vt = lp.sb(sF, "vt", [128, 64 * 36], BF16)
        ob = lp.sb(sF, "ob", [32, SEQ], BF16)
        ptile = lp.sb(sF, "ptile", [128, 2 * 512], BF16)
        otmp2 = lp.sb(sF, "otmp2", [128, 512], F32)
        rcp2 = lp.sb(sF, "rcp2", [1, 512], F32)
        for p in range(4):
            lda = dma_in(lp, [
                (qT[:, :], qi_i[:, p * SEQ:(p + 1) * SEQ]),
                (kT[:, :], ki_i[:, p * SEQ:(p + 1) * SEQ]),
                (vt[:, :], vi_i[:, p * 64 * 36:(p + 1) * 64 * 36]),
            ], engine="sync")
            wait_stage(lp, "tensor", "ld", lda)
            wait_stage(lp, "scalar", "ld", lda)
            if p > 0:
                wait_stage(lp, "vector", "st", lp.cnt["st"])
            emit_flash(lp, qT=qT[:, :], kT=kT[:, :],
                       v_tm=lambda kt: vt[:, kt * 36:kt * 36 + 33],
                       out_sb=ob[:, :], scale=0.125, Tq=SEQ, dva=33,
                       cmask=cmask[:, :], ones_r=ones_r[:, :],
                       P0=P0, P1=P1, P2=P2, ptile=ptile, otmp=otmp2, rcp=rcp2,
                       name="fl")
            dma_in(lp, [(o_in[p * 32:(p + 1) * 32, :], ob[:, :])],
                   engine="sync", sem="st")
    wait_stage(lp, "vector", "st", lp.cnt["st"])
    return lp



def build_launch_F():
    lp = LP("F")
    oin = lp.dram_in("oin", [128, 4 * T], BF16)
    wo_l = lp.dram_in("wo_l", [128, 4 * 1024], BF16)
    xn2f = lp.dram_in("xn2f", [128, KD * T], F32)
    x2 = lp.dram_in("x2", [128, KD * T], F32)
    s2 = lp.dram_in("s2", [128, KD * T], F32)
    wln2 = lp.dram_in("wln2", [128, 8], F32)
    wgate_l = lp.dram_in("wgate_l", [128, 8 * 8], F32)
    ebp = lp.dram_in("ebp", [8, 1], F32)
    swg = lp.dram_in("swg", [128, 8 * 512], BF16)
    swu = lp.dram_in("swu", [128, 8 * 512], BF16)
    swd = lp.dram_in("swd", [128, 4 * 1024], BF16)
    rwg = lp.dram_in("rwg", [128, 8 * 8 * 512], BF16)
    rwu = lp.dram_in("rwu", [128, 8 * 8 * 512], BF16)
    rwd = lp.dram_in("rwd", [128, 8 * 4 * 1024], BF16)
    ident = lp.dram_in("ident", [128, 128], F32)
    o_x3 = lp.dram_out("x3", [128, KD * T], F32)
    o_xn3 = lp.dram_out("xn3", [128, KD * T], BF16)
    lp.sem("st"); lp.sem("ld"); lp.sem("gld")
    lp.setup_consts([1e-5])
    P0 = lp.psum("P0", [128, 512])
    P1 = lp.psum("P1", [128, 512])
    P2 = lp.psum("P2", [128, 512])
    xb = lp.sbuf("xb", [128, KD * 512], BF16)
    sq = lp.sbuf("sq", [128, KD * 512], BF16)
    sS1 = lp.sbuf("sS1", [1, 512], F32)
    sS2 = lp.sbuf("sS2", [1, 512], F32)
    sAR = lp.sbuf("sAR", [1, 512], F32)
    sBR = lp.sbuf("sBR", [1, 512], F32)
    sM2 = lp.sbuf("sM2", [1, 512], F32)
    ones_c = lp.sbuf("ones_c", [128, 2], BF16)
    ones_r = lp.sbuf("ones_r", [1, 128], F32)
    ones1T = lp.sbuf("ones1T", [1, T], BF16)
    idnf = lp.sbuf("idnf", [128, 128], F32)
    dummy_scr_anchor = None
    scr = {"xb": xb, "sq": sq, "ones_col": ones_c[:, 0:1],
           "ones_row": ones_r[:, :], "P0": P0, "P1": P1, "PS": P2[0:1, :],
           "S1": sS1, "S2": sS2, "AR": sAR, "BR": sBR, "M2": sM2}
    onesb = lp.sbuf("onesb", [1, 128], BF16)
    rwE = lp.sbuf("rwE", [1, 512], BF16)
    wg_s = lp.sbuf("wg_s", [128, 8 * 8], F32)
    eb_s = lp.sbuf("eb_s", [8, 1], F32)
    gam2 = lp.sbuf("gam2", [128, 8], F32)
    yacc = lp.sbuf("yacc", [128, KD * T], F32)
    tt1 = lp.sbuf("tt1", [128, 512], F32)
    tt2 = lp.sbuf("tt2", [128, 512], F32)
    sP = lp.scope()
    sP.__enter__()
    wo_s = lp.sb(sP, "wo_s", [128, 4 * 1024], BF16)
    xrb = lp.sb(sP, "xrb", [128, KD * T], BF16)
    lgt = lp.sb(sP, "lgt", [8, T], F32)
    xi_scr = lp.dram_scratch("xi_scr", [128, KD * T], F32)
    ld1 = dma_in(lp, [(wg_s[:, :], wgate_l), (eb_s[:, :], ebp),
                      (gam2[:, :], wln2), (idnf[:, :], ident)], engine="sync")
    lw1 = dma_in(lp, [(wo_s[:, :], wo_l)], engine="gpsimd")
    with lp.block() as blk:
        def vinit(e, ld1=ld1, lw1=lw1):
            e.memset(ones_c[:, :], 1.0)
            e.memset(ones_r[:, :], 1.0)
            e.memset(ones1T[:, :], 1.0)
            e.memset(onesb[:, :], 1.0)
            e.wait_ge(lp.sem("ld"), ld1)
            e.wait_ge(lp.sem("gld"), lw1)
        blk.vector(vinit)

    for n in range(4):
        tsl = slice(n * 512, (n + 1) * 512)
        if n > 0:
            wait_stage(lp, "vector", "st", lp.cnt["st"])
            wait_stage(lp, "scalar", "st", lp.cnt["st"])
        with lp.scope() as sA:
            oic = lp.sb(sA, "oic", [128, 4 * 512], BF16)
            xnc = lp.sb(sA, "xnc", [128, KD * 512], F32)
            xic = lp.sb(sA, "xic", [128, KD * 512], F32)
            xrfc = lp.sb(sA, "xrfc", [128, KD * 512], F32)
            ld2 = dma_in(lp, [
                (oic[:, :], oin.rearrange("p (k t) -> p k t", t=T)[:, :, tsl]),
                (xnc[:, :], xn2f.rearrange("p (k t) -> p k t", t=T)[:, :, tsl]),
            ], engine="sync")
            wait_stage(lp, "tensor", "ld", ld2)
            wait_stage(lp, "vector", "ld", ld2)

            def at_ev(m, x, ps, e):
                return e.activation(xic[:, m * 512:(m + 1) * 512], ps[:, :],
                                    AF.Identity)
            mm_stage(lp, name="mm",
                     lhsT=lambda k, m: wo_s[:, k * 1024 + m * 128:k * 1024 + (m + 1) * 128],
                     rhs=lambda k, x: oic[:, k * 512:(k + 1) * 512],
                     evac=at_ev, tiles=[(m, 0) for m in range(8)],
                     k_tiles=4, psA=P0, psB=P1)
            with lp.block() as blk:
                def vadd(e):
                    for k in range(KD):
                        e.tensor_tensor(xic[:, k * 512:(k + 1) * 512],
                                        xic[:, k * 512:(k + 1) * 512],
                                        xnc[:, k * 512:(k + 1) * 512], ALU.add)
                blk.vector(vadd)
            dma_in(lp, [(xi_scr.rearrange("p (k t) -> p k t", t=T)[:, :, tsl], xic[:, :])],
                   engine="sync", sem="st")

            def xr_apply(k, x, sl2, e, n=n):
                e.tensor_tensor(tt1[:, :], xic[:, k * 512:(k + 1) * 512],
                                P0[:, 0:512], ALU.mult)
                e.tensor_scalar(xrfc[:, k * 512:(k + 1) * 512], tt1[:, :],
                                gam2[:, k:k + 1], None, ALU.mult)
            norm_chunked(lp, x_of=lambda k: xic[:, k * 512:(k + 1) * 512],
                         out_emit=xr_apply, scr=scr, kind="rms", name="nrm",
                         t_len=512, chunk=512)
            with lp.block() as blk:
                def vcast(e, n=n):
                    for k in range(KD):
                        e.tensor_copy(xrb[:, k * T + n * 512:k * T + (n + 1) * 512],
                                      xrfc[:, k * 512:(k + 1) * 512])
                blk.vector(vcast)

            def lg_ev(m, x, ps, e, n=n):
                return e.activation(lgt[:, n * 512:(n + 1) * 512], ps[0:8, :],
                                    AF.Identity, bias=eb_s[:, 0:1])
            mm_stage(lp, name="mm",
                     lhsT=lambda k, m: wg_s[:, k * 8:(k + 1) * 8],
                     rhs=lambda k, x: xrfc[:, k * 512:(k + 1) * 512],
                     evac=lg_ev, tiles=[(0, 0)], k_tiles=8, psA=P0, psB=P1,
                     out_part=8)
    wait_stage(lp, "vector", "st", lp.cnt["st"])

    rwT = lp.sb(sP, "rwT8", [8, T], BF16)
    with lp.scope() as sM:
        prball = lp.sb(sM, "prball", [128, 16 * 8], F32)
        seall = lp.sb(sM, "seall", [128, 16 * 8], F32)
        etall = lp.sb(sM, "etall", [128, 16 * 8], F32)
        rw_all = lp.sb(sM, "rw_all", [128, 16 * 8], F32)
        mxall = lp.sb(sM, "mxall", [128, 32], F32)
        lp.sem("tr")
        for tt in range(16):
            btr = lp.cnt["tr"]
            with lp.block() as blk:
                def tb(e, tt=tt, btr=btr):
                    e.transpose(P2[:, 0:8], lgt[0:8, tt * 128:(tt + 1) * 128],
                                idnf[0:8, 0:8])
                    lp.finc_t(e, lp.sem("tr"), 1)
                blk.tensor(tb)

                def vb(e, tt=tt, btr=btr):
                    e.wait_ge(lp.sem("tr"), btr + 1)
                    e.tensor_copy(prball[:, tt * 8:(tt + 1) * 8], P2[:, 0:8])
                blk.vector(vb)
            lp.bump("tr", 1)
        W3 = lambda ap: ap.rearrange("p (t e) -> p t e", e=8)
        W4 = lambda ap, w: ap.rearrange("p (t e) -> p t e", e=w)
        with lp.block() as blk:
            def vr(e):
                lg3 = W3(prball[:, :])
                e.tensor_tensor(W4(seall[:, 0:64], 4), lg3[:, :, 0:4], lg3[:, :, 4:8], ALU.max)
                lp.pad_v(e)
                e.tensor_tensor(W4(seall[:, 64:96], 2), W4(seall[:, 0:64], 4)[:, :, 0:2],
                                W4(seall[:, 0:64], 4)[:, :, 2:4], ALU.max)
                lp.pad_v(e)
                e.tensor_tensor(W4(mxall[:, 0:16], 1), W4(seall[:, 64:96], 2)[:, :, 0:1],
                                W4(seall[:, 64:96], 2)[:, :, 1:2], ALU.max)
                lp.pad_v(e)
                lp.pad_v(e)
                e.tensor_tensor(W3(etall[:, :]), lg3,
                                W4(mxall[:, 0:16], 1).to_broadcast([128, 16, 8]), ALU.subtract)
            blk.vector(vr)
        with lp.block() as blk:
            def sc(e):
                e.activation(etall[:, :], etall[:, :], AF.Exp)
            blk.scalar(sc)
        with lp.block() as blk:
            def v2(e):
                et3 = W3(etall[:, :])
                e.tensor_tensor(W4(seall[:, 0:64], 4), et3[:, :, 0:4], et3[:, :, 4:8], ALU.add)
                lp.pad_v(e)
                e.tensor_tensor(W4(seall[:, 64:96], 2), W4(seall[:, 0:64], 4)[:, :, 0:2],
                                W4(seall[:, 0:64], 4)[:, :, 2:4], ALU.add)
                lp.pad_v(e)
                e.tensor_tensor(W4(mxall[:, 0:16], 1), W4(seall[:, 64:96], 2)[:, :, 0:1],
                                W4(seall[:, 64:96], 2)[:, :, 1:2], ALU.add)
                lp.pad_v(e)
                e.reciprocal(mxall[:, 16:32], mxall[:, 0:16])
                lp.pad_v(e)
                lp.pad_v(e)
                e.tensor_tensor(W3(prball[:, :]), et3,
                                W4(mxall[:, 16:32], 1).to_broadcast([128, 16, 8]), ALU.mult)
                lp.pad_v(e)
                bi3 = W3(prball[:, :])
                e.tensor_tensor(W4(seall[:, 0:64], 4), bi3[:, :, 0:4], bi3[:, :, 4:8], ALU.max)
                lp.pad_v(e)
                e.tensor_tensor(W4(seall[:, 64:96], 2), W4(seall[:, 0:64], 4)[:, :, 0:2],
                                W4(seall[:, 0:64], 4)[:, :, 2:4], ALU.max)
                lp.pad_v(e)
                e.tensor_tensor(W4(mxall[:, 0:16], 1), W4(seall[:, 64:96], 2)[:, :, 0:1],
                                W4(seall[:, 64:96], 2)[:, :, 1:2], ALU.max)
                lp.pad_v(e)
                e.tensor_tensor(W3(seall[:, :]), bi3,
                                W4(mxall[:, 0:16], 1).to_broadcast([128, 16, 8]), ALU.is_ge)
                lp.pad_v(e)
                e.tensor_scalar(rw_all[:, :], seall[:, :], 1e30, None, ALU.mult)
                lp.pad_v(e)
                e.tensor_tensor(etall[:, :], prball[:, :], rw_all[:, :], ALU.subtract)
                lp.pad_v(e)
                bi2 = W3(etall[:, :])
                e.tensor_tensor(W4(rw_all[:, 0:64], 4), bi2[:, :, 0:4], bi2[:, :, 4:8], ALU.max)
                lp.pad_v(e)
                e.tensor_tensor(W4(rw_all[:, 64:96], 2), W4(rw_all[:, 0:64], 4)[:, :, 0:2],
                                W4(rw_all[:, 0:64], 4)[:, :, 2:4], ALU.max)
                lp.pad_v(e)
                e.tensor_tensor(W4(mxall[:, 0:16], 1), W4(rw_all[:, 64:96], 2)[:, :, 0:1],
                                W4(rw_all[:, 64:96], 2)[:, :, 1:2], ALU.max)
                lp.pad_v(e)
                e.tensor_tensor(W3(rw_all[:, :]), bi2,
                                W4(mxall[:, 0:16], 1).to_broadcast([128, 16, 8]), ALU.is_ge)
                lp.pad_v(e)
                e.tensor_tensor(seall[:, :], seall[:, :], rw_all[:, :], ALU.add)
                lp.pad_v(e)
                e.tensor_tensor(rw_all[:, :], seall[:, :], prball[:, :], ALU.mult)
                lp.pad_v(e)
                rw3 = W3(rw_all[:, :])
                e.tensor_tensor(W4(etall[:, 0:64], 4), rw3[:, :, 0:4], rw3[:, :, 4:8], ALU.add)
                lp.pad_v(e)
                e.tensor_tensor(W4(etall[:, 64:96], 2), W4(etall[:, 0:64], 4)[:, :, 0:2],
                                W4(etall[:, 0:64], 4)[:, :, 2:4], ALU.add)
                lp.pad_v(e)
                e.tensor_tensor(W4(mxall[:, 0:16], 1), W4(etall[:, 64:96], 2)[:, :, 0:1],
                                W4(etall[:, 64:96], 2)[:, :, 1:2], ALU.add)
                lp.pad_v(e)
                e.reciprocal(mxall[:, 16:32], mxall[:, 0:16])
                lp.pad_v(e)
                lp.pad_v(e)
                e.tensor_tensor(rw3, rw3,
                                W4(mxall[:, 16:32], 1).to_broadcast([128, 16, 8]), ALU.mult)
            blk.vector(v2)
        for tt in range(16):
            btr2 = lp.cnt["tr"]
            with lp.block() as blk:
                def tb2(e, tt=tt, btr2=btr2):
                    e.transpose(P2[0:8, 0:128], rw_all[:, tt * 8:(tt + 1) * 8], idnf[:, :])
                    lp.finc_t(e, lp.sem("tr"), 1)
                blk.tensor(tb2)

                def v3(e, tt=tt, btr2=btr2):
                    e.wait_ge(lp.sem("tr"), btr2 + 1)
                    e.tensor_copy(rwT[0:8, tt * 128:(tt + 1) * 128], P2[0:8, 0:128])
                blk.vector(v3)
            lp.bump("tr", 1)

    with lp.block() as blk:
        def vz(e):
            for k in range(KD):
                e.memset(yacc[:, k * T:(k + 1) * T], 0.0)
        blk.vector(vz)
    for shared in (True, False):
        wgsrc, wusrc, wdsrc = (swg, swu, swd) if shared else (rwg, rwu, rwd)
        for ee in range(1 if shared else 8):
            with lp.scope() as sE:
                wgc = lp.sb(sE, "wgc", [128, 8 * 512], BF16)
                wuc = lp.sb(sE, "wuc", [128, 8 * 512], BF16)
                wdc = lp.sb(sE, "wdc", [128, 4 * 1024], BF16)
                hbuf = lp.sb(sE, "hbuf", [128, 4 * 512], BF16)
                t1f = lp.sb(sE, "t1f", [128, 512], F32)
                t2f = lp.sb(sE, "t2f", [128, 512], F32)
                rwb = lp.sb(sE, "rwb", [128, 512], BF16)
                lw = dma_in(lp, [
                    (wgc[:, :], wgsrc[:, ee * 4096:(ee + 1) * 4096]),
                    (wuc[:, :], wusrc[:, ee * 4096:(ee + 1) * 4096]),
                    (wdc[:, :], wdsrc[:, ee * 4096:(ee + 1) * 4096]),
                ], engine="gpsimd")
                wait_stage(lp, "tensor", "gld", lw)
                for n in range(4):
                    if not shared:
                        lde = dma_in(lp, [(rwE[0:1, :],
                                           rwT[ee:ee + 1, n * 512:(n + 1) * 512])],
                                     engine="sync")
                        wait_stage(lp, "tensor", "ld", lde)
                        bb = lp.cnt["tr"]
                        with lp.block() as blk:
                            def tbr(e, bb=bb):
                                e.matmul(P2[:, 0:512], onesb[0:1, :], rwE[0:1, :],
                                         start=True, stop=True)
                                lp.finc_t(e, lp.sem("tr"), 1)
                            blk.tensor(tbr)

                            def vbr(e, bb=bb):
                                e.wait_ge(lp.sem("tr"), bb + 1)
                                e.tensor_copy(rwb[:, :], P2[:, 0:512])
                            blk.vector(vbr)
                        lp.bump("tr", 1)
                    gu = lp.sem("gu")
                    for mt in range(4):
                        base = lp.cnt["gu"]
                        with lp.block() as blk:
                            def tb(e, mt=mt, n=n, base=base):
                                for k in range(KD):
                                    e.matmul(P0[:, 0:512],
                                             wgc[:, k * 512 + mt * 128:k * 512 + (mt + 1) * 128],
                                             xrb[:, k * T + n * 512:k * T + (n + 1) * 512],
                                             start=(k == 0), stop=(k == KD - 1))
                                for k in range(KD):
                                    e.matmul(P1[:, 0:512],
                                             wuc[:, k * 512 + mt * 128:k * 512 + (mt + 1) * 128],
                                             xrb[:, k * T + n * 512:k * T + (n + 1) * 512],
                                             start=(k == 0), stop=(k == KD - 1))
                                lp.finc_t(e, gu, 1)
                            blk.tensor(tb)

                            def sb_(e, mt=mt, base=base):
                                e.wait_ge(gu, base + 1)
                                e.activation(t1f[:, :], P0[:, 0:512], AF.Silu)
                                lp.finc_s(e, gu, 1)
                            blk.scalar(sb_)

                            def vb(e, mt=mt, shared=shared, base=base):
                                e.wait_ge(gu, base + 2)
                                e.tensor_copy(t2f[:, :], P1[:, 0:512])
                                lp.pad_v(e)
                                if shared:
                                    e.tensor_tensor(hbuf[:, mt * 512:(mt + 1) * 512],
                                                    t1f[:, :], t2f[:, :], ALU.mult)
                                else:
                                    e.tensor_tensor(t1f[:, :], t1f[:, :], t2f[:, :],
                                                    ALU.mult)
                                    lp.pad_v(e)
                                    e.tensor_tensor(hbuf[:, mt * 512:(mt + 1) * 512],
                                                    t1f[:, :], rwb[:, :], ALU.mult)
                            blk.vector(vb)
                        lp.bump("gu", 2)
                    for mt in range(8):
                        base = lp.cnt["gu"]
                        with lp.block() as blk:
                            def tb2(e, mt=mt, base=base):
                                for k in range(4):
                                    e.matmul(P0[:, 0:512],
                                             wdc[:, k * 1024 + mt * 128:k * 1024 + (mt + 1) * 128],
                                             hbuf[:, k * 512:(k + 1) * 512],
                                             start=(k == 0), stop=(k == 3))
                                lp.finc_t(e, gu, 1)
                            blk.tensor(tb2)

                            def sb2(e, mt=mt, shared=shared, base=base):
                                e.wait_ge(gu, base + 1)
                                e.activation(t1f[:, :], P0[:, 0:512], AF.Identity)
                                lp.finc_s(e, gu, 1)
                            blk.scalar(sb2)

                            def vb2(e, mt=mt, n=n, base=base):
                                e.wait_ge(gu, base + 2)
                                e.tensor_tensor(yacc[:, mt * T + n * 512:mt * T + (n + 1) * 512],
                                                yacc[:, mt * T + n * 512:mt * T + (n + 1) * 512],
                                                t1f[:, :], ALU.add)
                            blk.vector(vb2)
                        lp.bump("gu", 2)

    sP.__exit__(None, None, None)
    for n in range(4):
        tsl = slice(n * 512, (n + 1) * 512)
        if n > 0:
            wait_stage(lp, "vector", "st", lp.cnt["st"])
            wait_stage(lp, "scalar", "st", lp.cnt["st"])
        with lp.scope() as sG:
            xic2 = lp.sb(sG, "xic2", [128, KD * 512], F32)
            x2c = lp.sb(sG, "x2c", [128, KD * 512], F32)
            s2c = lp.sb(sG, "s2c", [128, KD * 512], F32)
            g2c = lp.sb(sG, "g2c", [128, KD * 512], F32)
            x3c = lp.sb(sG, "x3c", [128, KD * 512], F32)
            xn3c = lp.sb(sG, "xn3c", [128, KD * 512], BF16)
            ld3 = dma_in(lp, [
                (xic2[:, :], xi_scr.rearrange("p (k t) -> p k t", t=T)[:, :, tsl]),
                (x2c[:, :], x2.rearrange("p (k t) -> p k t", t=T)[:, :, tsl]),
                (s2c[:, :], s2.rearrange("p (k t) -> p k t", t=T)[:, :, tsl]),
            ], engine="sync")
            wait_stage(lp, "vector", "ld", ld3)
            with lp.block() as blk:
                def vg(e, n=n):
                    for k in range(KD):
                        e.tensor_tensor(g2c[:, k * 512:(k + 1) * 512],
                                        xic2[:, k * 512:(k + 1) * 512],
                                        yacc[:, k * T + n * 512:k * T + (n + 1) * 512],
                                        ALU.add)
                blk.vector(vg)
            with lp.block() as blk:
                def sg(e):
                    for k in range(KD):
                        e.activation(g2c[:, k * 512:(k + 1) * 512],
                                     g2c[:, k * 512:(k + 1) * 512], AF.Silu)
                blk.scalar(sg)
            with lp.block() as blk:
                def vx3(e):
                    for k in range(KD):
                        e.tensor_tensor(tt1[:, :], s2c[:, k * 512:(k + 1) * 512],
                                        g2c[:, k * 512:(k + 1) * 512], ALU.mult)
                        e.tensor_tensor(x3c[:, k * 512:(k + 1) * 512],
                                        x2c[:, k * 512:(k + 1) * 512],
                                        tt1[:, :], ALU.add)
                blk.vector(vx3)
            dma_in(lp, [(o_x3.rearrange("p (k t) -> p k t", t=T)[:, :, tsl], x3c[:, :])],
                   engine="sync", sem="st")

            def xn3_apply(k, x, sl2, e):
                e.tensor_tensor(tt1[:, :], x3c[:, k * 512:(k + 1) * 512],
                                P0[:, 0:512], ALU.mult)
                e.tensor_tensor(xn3c[:, k * 512:(k + 1) * 512], tt1[:, :],
                                P1[:, 0:512], ALU.add)
            norm_chunked(lp, x_of=lambda k: x3c[:, k * 512:(k + 1) * 512],
                         out_emit=xn3_apply, scr=scr, kind="ln", name="nrm",
                         t_len=512, chunk=512)
            dma_in(lp, [(o_xn3.rearrange("p (k t) -> p k t", t=T)[:, :, tsl], xn3c[:, :])],
                   engine="sync", sem="st")
    wait_stage(lp, "sync", "st", lp.cnt["st"])
    return lp



def build_launch_F2():
    lp = LP("F2")
    xn3 = lp.dram_in("xn3", [128, KD * T], BF16)
    wdq = lp.dram_in("wdq", [128, 8 * 256], BF16)
    wuq = lp.dram_in("wuq", [128, 2 * 1024], BF16)
    wqr = lp.dram_in("wqr", [128, 2 * 512], BF16)
    wdkv = lp.dram_in("wdkv", [128, 8 * 256], BF16)
    wuk = lp.dram_in("wuk", [128, 2 * 1024], BF16)
    wuv = lp.dram_in("wuv", [128, 2 * 1024], BF16)
    wkr = lp.dram_in("wkr", [128, 8 * 64], BF16)
    fw1 = lp.dram_in("fw1", [128, 8 * 1024], BF16)
    fa1 = lp.dram_in("fa1", [128, 8 * 1024], BF16)
    fa2 = lp.dram_in("fa2", [128, 8 * 1024], BF16)
    fb = lp.dram_in("fb", [128, 8], F32)
    cos3 = lp.dram_in("cos3", [128, T], F32)
    sin3 = lp.dram_in("sin3", [128, T], F32)
    o_q3 = lp.dram_out("q3", [128, KD * T], BF16)
    o_qr3 = lp.dram_out("qr3", [128, 4 * T], BF16)
    o_k3 = lp.dram_out("k3", [128, KD * T], BF16)
    o_v3 = lp.dram_out("v3", [128, KD * T], BF16)
    o_kr3 = lp.dram_out("kr3", [64, T], BF16)
    o_g3 = lp.dram_out("g3", [128, KD * T], F32)
    lp.sem("st"); lp.sem("ld"); lp.sem("gld")
    lp.setup_consts([1e-5])
    P0 = lp.psum("P0", [128, 512])
    P1 = lp.psum("P1", [128, 512])
    wdq_s = lp.sbuf("wdq_s", [128, 8 * 256], BF16)
    wuq_s = lp.sbuf("wuq_s", [128, 2 * 1024], BF16)
    wqr_s = lp.sbuf("wqr_s", [128, 2 * 512], BF16)
    wdkv_s = lp.sbuf("wdkv_s", [128, 8 * 256], BF16)
    wuk_s = lp.sbuf("wuk_s", [128, 2 * 1024], BF16)
    wuv_s = lp.sbuf("wuv_s", [128, 2 * 1024], BF16)
    wkr_s = lp.sbuf("wkr_s", [128, 8 * 64], BF16)
    fw1_s = lp.sbuf("fw1_s", [128, 8 * 1024], BF16)
    fa1_s = lp.sbuf("fa1_s", [128, 8 * 1024], BF16)
    fa2_s = lp.sbuf("fa2_s", [128, 8 * 1024], BF16)
    fb_s = lp.sbuf("fb_s", [128, 8], F32)
    cos_s = lp.sbuf("cos_s", [128, T], F32)
    sin_s = lp.sbuf("sin_s", [128, T], F32)
    tt1 = lp.sbuf("tt1", [128, 512], F32)
    tt2 = lp.sbuf("tt2", [128, 512], F32)
    ld1 = dma_in(lp, [(fb_s[:, :], fb), (cos_s[:, :], cos3), (sin_s[:, :], sin3)],
                 engine="sync")
    lw1 = dma_in(lp, [(wdq_s[:, :], wdq), (wuq_s[:, :], wuq), (wqr_s[:, :], wqr),
                      (wdkv_s[:, :], wdkv), (wuk_s[:, :], wuk), (wuv_s[:, :], wuv),
                      (wkr_s[:, :], wkr), (fw1_s[:, :], fw1), (fa1_s[:, :], fa1),
                      (fa2_s[:, :], fa2)], engine="gpsimd")
    wait_stage(lp, "tensor", "ld", ld1)
    wait_stage(lp, "tensor", "gld", lw1)
    wait_stage(lp, "vector", "ld", ld1)
    for n in range(4):
        tsl = slice(n * 512, (n + 1) * 512)
        if n > 0:
            wait_stage(lp, "vector", "st", lp.cnt["st"])
            wait_stage(lp, "scalar", "st", lp.cnt["st"])
        with lp.scope() as sA:
            xnc = lp.sb(sA, "xnc", [128, KD * 512], BF16)
            cqb = lp.sb(sA, "cqb", [128, 2 * 512], BF16)
            ckb = lp.sb(sA, "ckb", [128, 2 * 512], BF16)
            q3c = lp.sb(sA, "q3c", [128, 8 * 512], BF16)
            qrc = lp.sb(sA, "qrc", [128, 4 * 512], F32)
            qrs = lp.sb(sA, "qrs", [128, 4 * 512], BF16)
            k3c = lp.sb(sA, "k3c", [128, 8 * 512], BF16)
            v3c = lp.sb(sA, "v3c", [128, 8 * 512], BF16)
            krc = lp.sb(sA, "krc", [64, 512], F32)
            kr2 = lp.sb(sA, "kr2", [16, 512], F32)
            krs = lp.sb(sA, "krs", [64, 512], BF16)
            h1b = lp.sb(sA, "h1b", [128, 8 * 512], BF16)
            h2b = lp.sb(sA, "h2b", [128, 8 * 512], BF16)
            g3c = lp.sb(sA, "g3c", [128, 8 * 512], F32)
            ld2 = dma_in(lp, [
                (xnc[:, :], xn3.rearrange("p (k t) -> p k t", t=T)[:, :, tsl]),
            ], engine="sync")
            wait_stage(lp, "tensor", "ld", ld2)
            mm_stage(lp, name="mm",
                     lhsT=lambda k, m: wdq_s[:, k * 256 + m * 128:k * 256 + (m + 1) * 128],
                     rhs=lambda k, x: xnc[:, k * 512:(k + 1) * 512],
                     evac=lambda m, x, ps, e: e.activation(
                         cqb[:, m * 512:(m + 1) * 512], ps[:, :], AF.Identity),
                     tiles=[(m, 0) for m in range(2)], k_tiles=8, psA=P0, psB=P1)
            mm_stage(lp, name="mm",
                     lhsT=lambda k, m: wdkv_s[:, k * 256 + m * 128:k * 256 + (m + 1) * 128],
                     rhs=lambda k, x: xnc[:, k * 512:(k + 1) * 512],
                     evac=lambda m, x, ps, e: e.activation(
                         ckb[:, m * 512:(m + 1) * 512], ps[:, :], AF.Identity),
                     tiles=[(m, 0) for m in range(2)], k_tiles=8, psA=P0, psB=P1)
            mm_stage(lp, name="mm",
                     lhsT=lambda k, m: wuq_s[:, k * 1024 + m * 128:k * 1024 + (m + 1) * 128],
                     rhs=lambda k, x: cqb[:, k * 512:(k + 1) * 512],
                     evac=lambda m, x, ps, e: e.activation(
                         q3c[:, m * 512:(m + 1) * 512], ps[:, :], AF.Identity),
                     tiles=[(m, 0) for m in range(8)], k_tiles=2, psA=P0, psB=P1)
            mm_stage(lp, name="mm",
                     lhsT=lambda k, m: wuk_s[:, k * 1024 + m * 128:k * 1024 + (m + 1) * 128],
                     rhs=lambda k, x: ckb[:, k * 512:(k + 1) * 512],
                     evac=lambda m, x, ps, e: e.activation(
                         k3c[:, m * 512:(m + 1) * 512], ps[:, :], AF.Identity),
                     tiles=[(m, 0) for m in range(8)], k_tiles=2, psA=P0, psB=P1)
            mm_stage(lp, name="mm",
                     lhsT=lambda k, m: wuv_s[:, k * 1024 + m * 128:k * 1024 + (m + 1) * 128],
                     rhs=lambda k, x: ckb[:, k * 512:(k + 1) * 512],
                     evac=lambda m, x, ps, e: e.activation(
                         v3c[:, m * 512:(m + 1) * 512], ps[:, :], AF.Identity),
                     tiles=[(m, 0) for m in range(8)], k_tiles=2, psA=P0, psB=P1)
            mm_stage(lp, name="mm",
                     lhsT=lambda k, m: wqr_s[:, k * 512 + m * 128:k * 512 + (m + 1) * 128],
                     rhs=lambda k, x: cqb[:, k * 512:(k + 1) * 512],
                     evac=lambda m, x, ps, e: e.activation(
                         qrc[:, m * 512:(m + 1) * 512], ps[:, :], AF.Identity),
                     tiles=[(m, 0) for m in range(4)], k_tiles=2, psA=P0, psB=P1)
            mm_stage(lp, name="mm",
                     lhsT=lambda k, m: wkr_s[:, k * 64:(k + 1) * 64],
                     rhs=lambda k, x: xnc[:, k * 512:(k + 1) * 512],
                     evac=lambda m, x, ps, e: e.activation(krc[:, :], ps[0:64, :],
                                                           AF.Identity),
                     tiles=[(0, 0)], k_tiles=8, psA=P0, psB=P1, out_part=64)
            cosc = cos_s[:, tsl]
            sinc = sin_s[:, tsl]
            with lp.block() as blk:
                def vrope(e):
                    for j in range(2):
                        x0 = qrc[:, j * 512:(j + 1) * 512]
                        x1_ = qrc[:, (2 + j) * 512:(3 + j) * 512]
                        e.tensor_tensor(tt1[:, :], x0, cosc[:, :], ALU.mult)
                        e.tensor_tensor(tt2[:, :], x1_, sinc[:, :], ALU.mult)
                        e.tensor_tensor(qrs[:, j * 512:(j + 1) * 512],
                                        tt1[:, :], tt2[:, :], ALU.subtract)
                        e.tensor_tensor(tt1[:, :], x0, sinc[:, :], ALU.mult)
                        e.tensor_tensor(tt2[:, :], x1_, cosc[:, :], ALU.mult)
                        e.tensor_tensor(qrs[:, (2 + j) * 512:(3 + j) * 512],
                                        tt1[:, :], tt2[:, :], ALU.add)
                    e.tensor_copy(kr2[:, :], krc[32:48, :])
                    lp.pad_v(e)
                    e.tensor_tensor(tt1[0:16, :], krc[0:16, :], cosc[0:16, :], ALU.mult)
                    e.tensor_tensor(tt2[0:16, :], kr2[:, :], sinc[0:16, :], ALU.mult)
                    lp.pad_v(e)
                    e.tensor_tensor(krs[0:16, :], tt1[0:16, :], tt2[0:16, :], ALU.subtract)
                    e.tensor_tensor(tt1[0:16, :], krc[0:16, :], sinc[0:16, :], ALU.mult)
                    e.tensor_tensor(tt2[0:16, :], kr2[:, :], cosc[0:16, :], ALU.mult)
                    lp.pad_v(e)
                    e.tensor_tensor(krs[32:48, :], tt1[0:16, :], tt2[0:16, :], ALU.add)
                blk.vector(vrope)
            dma_in(lp, [
                (o_q3.rearrange("p (k t) -> p k t", t=T)[:, :, tsl], q3c[:, :]),
                (o_qr3.rearrange("p (k t) -> p k t", t=T)[:, :, tsl], qrs[:, :]),
                (o_k3.rearrange("p (k t) -> p k t", t=T)[:, :, tsl], k3c[:, :]),
                (o_v3.rearrange("p (k t) -> p k t", t=T)[:, :, tsl], v3c[:, :]),
                (o_kr3[:, tsl], krs[:, :]),
            ], engine="sync", sem="st")
            mm_stage(lp, name="mm",
                     lhsT=lambda k, m: fw1_s[:, k * 1024 + m * 128:k * 1024 + (m + 1) * 128],
                     rhs=lambda k, x: xnc[:, k * 512:(k + 1) * 512],
                     evac=lambda m, x, ps, e: e.activation(
                         h1b[:, m * 512:(m + 1) * 512], ps[:, :], AF.Identity),
                     tiles=[(m, 0) for m in range(8)], k_tiles=8, psA=P0, psB=P1)
            mm_stage(lp, name="mm",
                     lhsT=lambda k, m: fa1_s[:, k * 1024 + m * 128:k * 1024 + (m + 1) * 128],
                     rhs=lambda k, x: h1b[:, k * 512:(k + 1) * 512],
                     evac=lambda m, x, ps, e: e.activation(
                         h2b[:, m * 512:(m + 1) * 512], ps[:, :], AF.Identity),
                     tiles=[(m, 0) for m in range(8)], k_tiles=8, psA=P0, psB=P1)

            def g3_ev(m, x, ps, e):
                e.activation(g3c[:, m * 512:(m + 1) * 512], ps[:, :], AF.Tanh,
                             bias=fb_s[:, m:m + 1])
                lp.pad_s(e)
                return e.activation(g3c[:, m * 512:(m + 1) * 512].bitcast(F32)
                                    if False else g3c[:, m * 512:(m + 1) * 512],
                                    g3c[:, m * 512:(m + 1) * 512], AF.Silu)
            mm_stage(lp, name="mm",
                     lhsT=lambda k, m: fa2_s[:, k * 1024 + m * 128:k * 1024 + (m + 1) * 128],
                     rhs=lambda k, x: h2b[:, k * 512:(k + 1) * 512],
                     evac=g3_ev, tiles=[(m, 0) for m in range(8)], k_tiles=8,
                     psA=P0, psB=P1)
            dma_in(lp, [(o_g3.rearrange("p (k t) -> p k t", t=T)[:, :, tsl], g3c[:, :])],
                   engine="sync", sem="st")
    wait_stage(lp, "sync", "st", lp.cnt["st"])
    return lp



def build_launch_G():
    lp = LP("G")
    q_i = lp.dram_in("q_i", [96, 4 * SEQ], BF16)
    k_i = lp.dram_in("k_i", [96, 4 * SEQ], BF16)
    v_i = lp.dram_in("v_i", [128, 4 * 64 * 68], BF16)
    cmask_i = lp.dram_in("cmask_i", [128, 4 * 512], F32)
    o_a = lp.dram_out("a3o", [64, 4 * SEQ], BF16)
    lp.sem("st"); lp.sem("ld"); lp.sem("gld")
    lp.setup_consts([1e-5, 80.0, 160.0])
    P0 = lp.psum("P0", [128, 512])
    P1 = lp.psum("P1", [128, 512])
    P2 = lp.psum("P2", [128, 512])
    ones_r = lp.sbuf("ones_r", [1, 128], F32)
    cmask = lp.sbuf("cmask", [128, 4 * 512], BF16)
    ldm = dma_in(lp, [(cmask[:, :], cmask_i)], engine="gpsimd")
    with lp.block() as blk:
        def vinit(e, ldm=ldm):
            e.memset(ones_r[:, :], 1.0)
            e.wait_ge(lp.sem("gld"), ldm)
        blk.vector(vinit)
    import math
    with lp.scope() as sA:
        qT = lp.sb(sA, "qT", [96, SEQ], BF16)
        kT = lp.sb(sA, "kT", [96, SEQ], BF16)
        vt = lp.sb(sA, "vt", [128, 64 * 68], BF16)
        ob = lp.sb(sA, "ob", [64, SEQ], BF16)
        ptile = lp.sb(sA, "ptile", [128, 2 * 512], BF16)
        otmp = lp.sb(sA, "otmp", [128, 512], F32)
        rcp = lp.sb(sA, "rcp", [1, 512], F32)
        ctile = lp.sb(sA, "ctile", [128, 512], F32)
        for p in range(4):
            lda = dma_in(lp, [
                (qT[:, :], q_i[:, p * SEQ:(p + 1) * SEQ]),
                (kT[:, :], k_i[:, p * SEQ:(p + 1) * SEQ]),
                (vt[:, :], v_i[:, p * 64 * 68:(p + 1) * 64 * 68]),
            ], engine="sync")
            wait_stage(lp, "tensor", "ld", lda)
            wait_stage(lp, "scalar", "ld", lda)
            if p > 0:
                wait_stage(lp, "vector", "st", lp.cnt["st"])
            emit_flash(lp, qT=qT[:, :], kT=kT[:, :],
                       v_tm=lambda kt: vt[:, kt * 68:kt * 68 + 65],
                       out_sb=ob[:, :], scale=1.0 / math.sqrt(96.0), Tq=SEQ,
                       dva=65, cmask=cmask[:, :], ones_r=ones_r[:, :],
                       P0=P0, P1=P1, P2=P2, ptile=ptile, otmp=otmp, rcp=rcp,
                       name="fl", clip80=True, ctile=ctile)
            dma_in(lp, [(o_a[:, p * SEQ:(p + 1) * SEQ], ob[:, :])],
                   engine="sync", sem="st")
    wait_stage(lp, "vector", "st", lp.cnt["st"])
    return lp



def build_launch_H():
    lp = LP("H")
    a3 = lp.dram_in("a3", [128, KD * T], BF16)
    wo3 = lp.dram_in("wo3", [128, 8 * 1024], BF16)
    x3 = lp.dram_in("x3", [128, KD * T], F32)
    g3 = lp.dram_in("g3", [128, KD * T], F32)
    o_x4 = lp.dram_out("x4", [128, KD * T], F32)
    lp.sem("st"); lp.sem("ld"); lp.sem("gld")
    lp.setup_consts([1e-5])
    P0 = lp.psum("P0", [128, 512])
    P1 = lp.psum("P1", [128, 512])
    wo_s = lp.sbuf("wo_s", [128, 8 * 1024], BF16)
    tt1 = lp.sbuf("tt1", [128, 512], F32)
    lw1 = dma_in(lp, [(wo_s[:, :], wo3)], engine="gpsimd")
    wait_stage(lp, "tensor", "gld", lw1)
    for n in range(4):
        tsl = slice(n * 512, (n + 1) * 512)
        if n > 0:
            wait_stage(lp, "vector", "st", lp.cnt["st"])
            wait_stage(lp, "scalar", "st", lp.cnt["st"])
        with lp.scope() as sA:
            a3c = lp.sb(sA, "a3c", [128, KD * 512], BF16)
            x3c = lp.sb(sA, "x3c", [128, KD * 512], F32)
            g3c = lp.sb(sA, "g3c", [128, KD * 512], F32)
            s3c = lp.sb(sA, "s3c", [128, KD * 512], F32)
            x4c = lp.sb(sA, "x4c", [128, KD * 512], F32)
            ld2 = dma_in(lp, [
                (a3c[:, :], a3.rearrange("p (k t) -> p k t", t=T)[:, :, tsl]),
                (x3c[:, :], x3.rearrange("p (k t) -> p k t", t=T)[:, :, tsl]),
                (g3c[:, :], g3.rearrange("p (k t) -> p k t", t=T)[:, :, tsl]),
            ], engine="sync")
            wait_stage(lp, "tensor", "ld", ld2)
            wait_stage(lp, "vector", "ld", ld2)
            mm_stage(lp, name="mm",
                     lhsT=lambda k, m: wo_s[:, k * 1024 + m * 128:k * 1024 + (m + 1) * 128],
                     rhs=lambda k, x: a3c[:, k * 512:(k + 1) * 512],
                     evac=lambda m, x, ps, e: e.activation(
                         s3c[:, m * 512:(m + 1) * 512], ps[:, :], AF.Identity),
                     tiles=[(m, 0) for m in range(8)], k_tiles=8, psA=P0, psB=P1)
            with lp.block() as blk:
                def vx4(e):
                    for k in range(KD):
                        e.tensor_tensor(tt1[:, :], s3c[:, k * 512:(k + 1) * 512],
                                        g3c[:, k * 512:(k + 1) * 512], ALU.mult)
                        e.tensor_tensor(x4c[:, k * 512:(k + 1) * 512],
                                        x3c[:, k * 512:(k + 1) * 512],
                                        tt1[:, :], ALU.add)
                blk.vector(vx4)
            dma_in(lp, [(o_x4.rearrange("p (k t) -> p k t", t=T)[:, :, tsl], x4c[:, :])],
                   engine="sync", sem="st")
    wait_stage(lp, "sync", "st", lp.cnt["st"])
    return lp



def _heads16_of_core(c):
    return [(p // 16, p % 16) for p in (4 * c, 4 * c + 1, 4 * c + 2, 4 * c + 3)]


def _unfm(a, kd):
    a = np.asarray(a)
    return a.reshape(128, kd, a.shape[1] // kd).transpose(1, 0, 2).reshape(kd * 128, -1)


def _wfm(w):
    w = np.asarray(w, np.float32)
    K, N = w.shape
    return np.ascontiguousarray(w.reshape(K // 128, 128, N).transpose(1, 0, 2)
                                .reshape(128, (K // 128) * N))


def _cmask_host():
    cm = np.zeros((128, 4 * 512), np.float32)
    for r in range(4):
        k = np.arange(128)[:, None] + r * 128
        q = np.arange(512)[None, :]
        cm[:, r * 512:(r + 1) * 512] = (k <= q).astype(np.float32)
    return cm


def _rope_tables(theta, off):
    i = np.arange(16, dtype=np.float64)
    inv = theta ** (-(2.0 * i) / 32.0)
    pos = off + np.arange(T, dtype=np.float64)
    ang = np.outer(inv, pos)
    cos = np.tile(np.cos(ang), (8, 1)).astype(np.float32)
    sin = np.tile(np.sin(ang), (8, 1)).astype(np.float32)
    return cos, sin


def prep_D(inp, Bo, Co):
    lo_full, go_full = [], []
    for b in range(2):
        lo = np.zeros((512, SEQ), np.float32)
        go = np.zeros((512, 2048), np.float32)
        for h in range(8):
            i = b * 8 + h
            cc, p = i // 2, i % 2
            lo[h * 64:(h + 1) * 64] = np.asarray(Co[cc]["lout"], np.float32)[p * 64:(p + 1) * 64]
            go[h * 64:(h + 1) * 64] = np.asarray(Co[cc]["gout"], np.float32)[p * 64:(p + 1) * 64]
        lo_full.append(lo); go_full.append(go)
    wlp = _wlay(inp["L1_attn_w_lproj"]).astype(BFD)
    blp = _pcol(inp["L1_attn_b_lproj"])
    wgp = _wlay(inp["L1_attn_w_gproj"]).astype(BFD)
    bgp = _pcol(inp["L1_attn_b_gproj"])
    wgt = _wlay(_f32(inp["L1_attn_w_gate"]))
    bg2 = _f32(inp["L1_attn_b_gate"]).reshape(1, 2)
    wqT = _wlay(_f32(inp["L2_attn_Wq"]).T).astype(BFD)
    bq = _pcol(inp["L2_attn_bq"])
    wkfm = _wfm(_f32(inp["L2_attn_Wk"]).T).astype(BFD)
    wvfm = _wfm(_f32(inp["L2_attn_Wv"]).T).astype(BFD)
    bkrow = _f32(inp["L2_attn_bk"]).reshape(1, 1024).astype(BFD)
    bvrow = _f32(inp["L2_attn_bv"]).reshape(1, 1024).astype(BFD)
    wln1 = _pcol(inp["L2_ffn_w_ln1"])
    Wqi = _f32(inp["L2_ffn_Wq"])
    order = ([h * 64 + j for h in range(16) for j in range(32)]
             + [h * 64 + 32 + 2 * i for h in range(16) for i in range(16)]
             + [h * 64 + 32 + 2 * i + 1 for h in range(16) for i in range(16)])
    wqi = _wlay(Wqi[:, order]).astype(BFD)
    Wkva = _f32(inp["L2_ffn_Wkva"])
    wkva_lat = _wlay(Wkva[:, 0:128]).astype(BFD)
    krm = np.zeros((1024, 64), np.float32)
    for i in range(16):
        krm[:, i] = Wkva[:, 128 + 2 * i]
        krm[:, 32 + i] = Wkva[:, 128 + 2 * i + 1]
    wkva_kr = _wlay(krm).astype(BFD)
    wkvb = _wlay(_f32(inp["L2_ffn_Wkvb"])).astype(BFD)
    maps = []
    for c in range(NCORES):
        b, r = c // 4, c % 4
        cos, sin = _rope_tables(100000.0, r * T)
        maps.append(dict(
            x1=np.asarray(Bo[c]["x1"]), g1b=np.asarray(Co[c]["g1"]),
            loutT=_fm(lo_full[b][:, r * T:(r + 1) * T].T).astype(BFD),
            goutT=_fm(go_full[b][:, r * 512:(r + 1) * 512].T).astype(BFD),
            wlp=wlp, blp=blp, wgp=wgp, bgp=bgp, wgt=wgt, bg2=bg2,
            wqT=wqT, bq=bq, wkfm=wkfm, wvfm=wvfm, bkrow=bkrow, bvrow=bvrow,
            wln1=wln1, wqi=wqi, wkva_lat=wkva_lat, wkva_kr=wkva_kr,
            wkvb=wkvb, cosI=cos, sinI=sin,
        ))
    return maps


def _unpack_kvp(a):
    a = np.asarray(a, np.float32)
    out = np.zeros((64, 16 * 65), np.float32)
    for h in range(16):
        bank = 0 if h < 8 else 1
        hh = h % 8
        rows = slice(64 * (hh % 2), 64 * (hh % 2) + 64)
        c0 = bank * 512 + (hh // 2) * 128
        out[:, h * 65:(h + 1) * 65] = a[rows, c0:c0 + 65]
    return out


def _pack_vtm(v, dva, stride):
    Tq, dv = v.shape
    va = np.zeros((Tq, stride), np.float32)
    va[:, 0:dv] = v
    va[:, dva - 1] = 1.0
    return np.ascontiguousarray(va.reshape(Tq // 128, 128, stride)
                                .transpose(1, 0, 2).reshape(128, -1))


def prep_E(inp, Do):
    wc_l = _wlay(_f32(inp["L2_attn_Wc"]).T).astype(BFD)
    bc = _pcol(inp["L2_attn_bc"])
    cmask = _cmask_host()
    rq_full, kvn_full, kr_full = [], [], []
    for b in range(2):
        rq_full.append(np.concatenate(
            [_unfm(Do[4 * b + r]["rq"], 8) for r in range(4)], -1).astype(np.float32))
        kvn_full.append(np.concatenate(
            [_unfm(Do[4 * b + r]["kvn"], 4) for r in range(4)], -1).astype(np.float32))
        kr_full.append(np.concatenate(
            [np.asarray(Do[4 * b + r]["kr"], np.float32) for r in range(4)], -1))
    maps = []
    for c in range(NCORES):
        b = c // 4
        kvp4 = np.concatenate([_unpack_kvp(Do[4 * b + r]["kvp"]) for r in range(4)], -1)
        qs, ks, vs = [], [], []
        for (bb, h) in _heads16_of_core(c):
            qT = np.concatenate([
                rq_full[bb][32 * h:32 * (h + 1), :],
                rq_full[bb][512 + 16 * h:512 + 16 * (h + 1), :],
                rq_full[bb][768 + 16 * h:768 + 16 * (h + 1), :]], 0)
            kT = np.concatenate([
                kvn_full[bb][32 * h:32 * (h + 1), :],
                kr_full[bb][0:16, :], kr_full[bb][32:48, :]], 0)
            qs.append(qT); ks.append(kT)
            vs.append(_pack_vtm(kvn_full[bb][32 * h:32 * (h + 1), :].T, 33, 36))
        maps.append(dict(
            q2=np.asarray(Do[c]["q2"]), kvp4=kvp4, wc_l=wc_l, bc=bc,
            qi_i=np.concatenate(qs, -1).astype(BFD),
            ki_i=np.concatenate(ks, -1).astype(BFD),
            vi_i=np.concatenate(vs, -1).astype(BFD),
            cmask_i=cmask,
        ))
    return maps


def prep_F(inp, Do, Eo):
    in_full = []
    for b in range(2):
        a = np.zeros((512, SEQ), np.float32)
        for h in range(16):
            i = b * 16 + h
            cc, p = i // 4, i % 4
            a[32 * h:32 * (h + 1)] = np.asarray(Eo[cc]["inq"], np.float32)[p * 32:(p + 1) * 32]
        in_full.append(a)
    Wo = _f32(inp["L2_ffn_Wo"])
    sel = [h * 64 + j for h in range(16) for j in range(32)]
    wo_l = _wlay(Wo[sel, :]).astype(BFD)
    wln2 = _pcol(inp["L2_ffn_w_ln2"])
    wgate_l = _wlay(_f32(inp["L2_ffn_Wgate"]))
    ebp = _f32(inp["L2_ffn_expert_bias"]).reshape(8, 1)
    def stack3(wg, wu, wd):
        ne = np.asarray(wg).shape[0]
        g = np.concatenate([_wlay(_f32(wg[e])) for e in range(ne)], 1).astype(BFD)
        u = np.concatenate([_wlay(_f32(wu[e])) for e in range(ne)], 1).astype(BFD)
        d = np.concatenate([_wlay(_f32(wd[e])) for e in range(ne)], 1).astype(BFD)
        return g, u, d
    swg, swu, swd = stack3(inp["L2_ffn_sWg"], inp["L2_ffn_sWu"], inp["L2_ffn_sWd"])
    rwg, rwu, rwd = stack3(inp["L2_ffn_Wg"], inp["L2_ffn_Wu"], inp["L2_ffn_Wd"])
    ident = np.eye(128, dtype=np.float32)
    maps = []
    for c in range(NCORES):
        b, r = c // 4, c % 4
        maps.append(dict(
            oin=_fm(in_full[b][:, r * T:(r + 1) * T].T).astype(BFD),
            wo_l=wo_l, xn2f=np.asarray(Do[c]["xn2f"]), x2=np.asarray(Do[c]["x2"]),
            s2=np.asarray(Eo[c]["s2"]), wln2=wln2, wgate_l=wgate_l, ebp=ebp,
            swg=swg, swu=swu, swd=swd, rwg=rwg, rwu=rwu, rwd=rwd, ident=ident,
        ))
    return maps


def prep_F2(inp, Fo):
    wdq = _wlay(inp["L3_attn_W_DQ"]).astype(BFD)
    wuq = _wlay(inp["L3_attn_W_UQ"]).astype(BFD)
    WQR = _f32(inp["L3_attn_W_QR"])
    order = ([32 * h + 2 * i for h in range(16) for i in range(16)]
             + [32 * h + 2 * i + 1 for h in range(16) for i in range(16)])
    wqr = _wlay(WQR[:, order]).astype(BFD)
    wdkv = _wlay(inp["L3_attn_W_DKV"]).astype(BFD)
    wuk = _wlay(inp["L3_attn_W_UK"]).astype(BFD)
    wuv = _wlay(inp["L3_attn_W_UV"]).astype(BFD)
    WKR = _f32(inp["L3_attn_W_KR"])
    krm = np.zeros((1024, 64), np.float32)
    for i in range(16):
        krm[:, i] = WKR[:, 2 * i]
        krm[:, 32 + i] = WKR[:, 2 * i + 1]
    wkr = _wlay(krm).astype(BFD)
    fw1 = _wlay(inp["L3_ffn_weight"]).astype(BFD)
    fa1 = _wlay(inp["L3_ffn_aux_weight1"]).astype(BFD)
    fa2 = _wlay(inp["L3_ffn_aux_weight2"]).astype(BFD)
    fb = _pcol(inp["L3_ffn_bias"])
    maps = []
    for c in range(NCORES):
        r = c % 4
        cos, sin = _rope_tables(10000.0, r * T)
        maps.append(dict(
            xn3=np.asarray(Fo[c]["xn3"]), wdq=wdq, wuq=wuq, wqr=wqr,
            wdkv=wdkv, wuk=wuk, wuv=wuv, wkr=wkr, fw1=fw1, fa1=fa1, fa2=fa2,
            fb=fb, cos3=cos, sin3=sin,
        ))
    return maps


def prep_G(inp, F2o):
    cmask = _cmask_host()
    q_full, qr_full, k_full, v_full, kr_full = [], [], [], [], []
    for b in range(2):
        q_full.append(np.concatenate(
            [_unfm(F2o[4 * b + r]["q3"], 8) for r in range(4)], -1).astype(np.float32))
        qr_full.append(np.concatenate(
            [_unfm(F2o[4 * b + r]["qr3"], 4) for r in range(4)], -1).astype(np.float32))
        k_full.append(np.concatenate(
            [_unfm(F2o[4 * b + r]["k3"], 8) for r in range(4)], -1).astype(np.float32))
        v_full.append(np.concatenate(
            [_unfm(F2o[4 * b + r]["v3"], 8) for r in range(4)], -1).astype(np.float32))
        kr_full.append(np.concatenate(
            [np.asarray(F2o[4 * b + r]["kr3"], np.float32) for r in range(4)], -1))
    maps = []
    for c in range(NCORES):
        qs, ks, vs = [], [], []
        for (bb, h) in _heads16_of_core(c):
            qT = np.concatenate([
                q_full[bb][64 * h:64 * (h + 1), :],
                qr_full[bb][16 * h:16 * (h + 1), :],
                qr_full[bb][256 + 16 * h:256 + 16 * (h + 1), :]], 0)
            kT = np.concatenate([
                k_full[bb][64 * h:64 * (h + 1), :],
                kr_full[bb][0:16, :], kr_full[bb][32:48, :]], 0)
            qs.append(qT); ks.append(kT)
            vs.append(_pack_vtm(v_full[bb][64 * h:64 * (h + 1), :].T, 65, 68))
        maps.append(dict(
            q_i=np.concatenate(qs, -1).astype(BFD),
            k_i=np.concatenate(ks, -1).astype(BFD),
            v_i=np.concatenate(vs, -1).astype(BFD),
            cmask_i=cmask,
        ))
    return maps


def prep_H(inp, Fo, F2o, Go):
    wo3 = _wlay(inp["L3_attn_W_O"]).astype(BFD)
    at_full = []
    for b in range(2):
        a = np.zeros((1024, SEQ), np.float32)
        for h in range(16):
            i = b * 16 + h
            cc, p = i // 4, i % 4
            a[64 * h:64 * (h + 1)] = np.asarray(Go[cc]["a3o"], np.float32)[:, p * SEQ:(p + 1) * SEQ]
        at_full.append(a)
    maps = []
    for c in range(NCORES):
        b, r = c // 4, c % 4
        maps.append(dict(
            a3=_fm(at_full[b][:, r * T:(r + 1) * T].T).astype(BFD),
            wo3=wo3, x3=np.asarray(Fo[c]["x3"]), g3=np.asarray(F2o[c]["g3"]),
        ))
    return maps


def kernel(**inputs):
    from concourse.bass_utils import run_bass_kernel_spmd
    cores = list(range(NCORES))

    def run(lp, maps):
        return run_bass_kernel_spmd(lp.nc, maps, core_ids=cores).results

    A = run(build_launch_A(), prep_A(inputs))
    Bo = run(build_launch_B(), prep_B(inputs, A))
    Co = run(build_launch_C(), prep_C(inputs, Bo))
    Do = run(build_launch_D(), prep_D(inputs, Bo, Co))
    Eo = run(build_launch_E(), prep_E(inputs, Do))
    Fo = run(build_launch_F(), prep_F(inputs, Do, Eo))
    F2o = run(build_launch_F2(), prep_F2(inputs, Fo))
    Go = run(build_launch_G(), prep_G(inputs, F2o))
    Ho = run(build_launch_H(), prep_H(inputs, Fo, F2o, Go))
    out = np.zeros((2, SEQ, D), np.float32)
    for c in range(NCORES):
        b, r = c // 4, c % 4
        a = np.asarray(Ho[c]["x4"], np.float32).reshape(128, KD, T)
        out[b, r * T:(r + 1) * T, :] = a.transpose(2, 1, 0).reshape(T, KD * 128)
    return out
